# version 09
# baseline (speedup 1.0000x reference)
import sys
import numpy as np
from contextlib import ExitStack

sys.path.insert(0, "/opt/trn_rl_repo")

import concourse.bass as bass
import concourse.mybir as mybir
from concourse.bass_utils import run_bass_kernel_spmd

F32 = mybir.dt.float32
BF16 = mybir.dt.bfloat16
AF = mybir.ActivationFunctionType
ALU = mybir.AluOpType
AX = mybir.AxisListType
ts = bass.ts

B, S, T, DM, NCH = 4, 4096, 2048, 1024, 8
H, DH = 16, 64
NEG = -1e30
EPS = 1e-5
FINNER, WIN, HID, NAB = 512, 512, 2730, 22
LAST = HID - 128 * (NAB - 1)
QK_SCALE = 8.0
SLOPES = [2.0 ** (-8.0 / (16 - h)) for h in range(H)]

N_FFN_LAYERS = 2
DEBUG_TAPS = 0


def host_constants(core):
    hh = core % 2
    q_base = T * hh
    other_base = T * (1 - hh)
    C = {}

    ident = np.eye(128, dtype=np.float32)
    C["c_ident"] = ident
    C["c_neg2id"] = -2.0 * ident
    C["c_ones_col"] = np.ones((128, 1), np.float32)
    C["c_ones_colf"] = np.ones((128, 1), np.float32)
    C["c_ones_row_f"] = np.ones((1, 128), np.float32)
    C["c_ones33"] = np.ones((33, 128), np.float32)
    blk2 = np.zeros((128, 2), np.float32)
    blk2[0:64, 0] = 1.0
    blk2[64:128, 1] = 1.0
    C["c_blk2"] = blk2
    ind2 = np.zeros((2, 128), np.float32)
    ind2[0, 0:64] = 1.0
    ind2[1, 64:128] = 1.0
    C["c_ind2"] = ind2

    al = np.zeros((35, 4 * 32, 128), np.float32)
    a_idx = np.arange(128, dtype=np.float32)
    b_idx = np.arange(512, dtype=np.float32)
    for qt in range(4):
        q0 = q_base + 512 * qt
        for kc in range(32):
            j0 = q_base + 128 * kc if kc < 16 else other_base + 128 * (kc - 16)
            v = qt * 32 + kc
            s = -1.0 if j0 >= q0 + 512 else 1.0
            Cc = q0 - j0 + 192
            al[0, v, :] = s
            al[1, v, :] = s * (a_idx - 64.0)
            al[2, v, :] = -s * Cc
            al[32:35, v, :] = al[0:3, v, :]
    C["c_alibi_lhsT"] = al.reshape(35, 4 * 32 * 128)

    rhs = np.zeros((35, 512), np.float32)
    rhs[0] = -(b_idx - 256.0)
    rhs[1] = 1.0
    rhs[2] = 1.0
    rhs[32:35] = rhs[0:3]
    C["c_alibi_rhs"] = rhs

    Rt = np.zeros((128, 4, 512), np.float32)
    for di, d in enumerate((0, 128, 256, 384)):
        Rt[:, di, :] = np.maximum(d + a_idx[:, None] - b_idx[None, :], 0.0)
    C["c_R"] = Rt.reshape(128, 4 * 512)

    qs = np.zeros((128, 8), np.float32)
    for p in range(8):
        qs[0:64, p] = 0.125 / SLOPES[2 * p]
        qs[64:128, p] = 0.125 / SLOPES[2 * p + 1]
    C["c_qscale"] = qs
    sq = np.zeros((128, 8), np.float32)
    for p in range(8):
        sq[0, p] = np.sqrt(SLOPES[2 * p])
        sq[1, p] = np.sqrt(SLOPES[2 * p + 1])
    C["c_sqslope"] = sq

    t_glob = np.arange(q_base, q_base + T, dtype=np.float64)
    half = np.arange(0, DH, 2, dtype=np.float64)
    inv_freq = 1.0 / 10000.0 ** (half / DH)
    freqs = np.concatenate([t_glob[:, None] * inv_freq[None, :]] * 2, axis=1)
    base = (half + 0.4 * DH) / (1.4 * DH)
    power = ((t_glob % WIN)) / float(WIN // 2)
    scale = np.concatenate([base[None, :] ** power[:, None]] * 2, axis=1)
    cos, sin = np.cos(freqs), np.sin(freqs)
    sgn = np.ones((DH,))
    sgn[0:32] = -1.0

    def dup(t):
        t64 = t.T
        return np.ascontiguousarray(
            np.concatenate([t64, t64], axis=0).astype(np.float32))

    C["c_rot_qc"] = dup(cos * scale) * QK_SCALE
    C["c_rot_qs"] = dup(sin * scale * sgn[None, :]) * QK_SCALE
    C["c_rot_kc"] = dup(cos / scale)
    C["c_rot_ks"] = dup(sin / scale * sgn[None, :])
    b2v = np.concatenate([base ** 2, base ** 2])
    C["c_base2"] = np.ascontiguousarray(
        np.concatenate([b2v, b2v]).astype(np.float32).reshape(128, 1))

    aa = a_idx[:, None]
    bb = np.arange(128, dtype=np.float32)[None, :]
    C["c_mask_d0"] = np.where(aa > bb, NEG, 0.0).astype(np.float32)
    C["c_mask_dm512"] = np.where(aa < bb, NEG, 0.0).astype(np.float32)
    C["c_mask_w0"] = np.full((128, 128), NEG if hh == 0 else 0.0, np.float32)
    C["c_gate"] = np.full((128, 1), float(hh), np.float32)
    return C


class Sem:
    def __init__(self, handle, name):
        self.h, self.name, self.n = handle, name, 0


class Plan:
    ENGINES = ("sync", "gpsimd", "tensor", "scalar", "vector")

    def __init__(self):
        self.items = []

    def op(self, eng, fn, waits=(), incs=(), nd=False):
        ws = []
        for w in waits:
            if isinstance(w, Sem):
                ws.append((w, w.n))
            else:
                ws.append((w[0], w[1]))
        incs = [i for i in incs if i[1] > 0]
        self.items.append((eng, ws, fn, incs, nd))
        for s, k in incs:
            s.n += k

    def emit_into(self, block):
        for name in self.ENGINES:
            if not any(e == name for e, _, _, _, _ in self.items):
                continue
            getattr(block, name)(lambda eng, name=name: self._emit(name, eng))

    def _emit(self, name, eng):
        autodrain = name in ("vector", "scalar")
        for e, ws, fn, incs, nd in self.items:
            if e != name:
                continue
            for s, v in ws:
                if v > 0:
                    eng.wait_ge(s.h, v)
            ins = fn(eng)
            for s, k in incs:
                ins = ins.then_inc(s.h, k)
            if autodrain and not nd:
                eng.drain()


def build(n_ffn=N_FFN_LAYERS, taps=DEBUG_TAPS):
    nc = bass.Bass("TRN2", target_bir_lowering=False, debug=False)
    DT = nc.dram_tensor

    def IN(name, shape):
        return DT(name, shape, F32, kind="ExternalInput").ap()

    xfm = IN("xfm", [DM, S])
    ln_gb = IN("ln_gb", [128, 2, NCH])
    Wq, Wk, Wv, Wo = (IN(n, [DM, DM]) for n in ("Wq", "Wk", "Wv", "Wo"))
    fw_qkv = [IN(f"fw_qkv{l}", [DM, 3 * FINNER]) for l in range(2)]
    f_qsv = [IN(f"f_qsv{l}", [128, 2]) for l in range(2)]
    fw_o = [IN(f"fw_o{l}", [FINNER, DM]) for l in range(2)]
    f_lgb = [IN(f"f_lgb{l}", [128, 2, NCH]) for l in range(2)]
    fw_1 = [IN(f"fw_1{l}", [DM, 2 * HID]) for l in range(2)]
    f_b1v = [IN(f"f_b1v{l}", [128, 2, NAB]) for l in range(2)]
    fw_2 = [IN(f"fw_2{l}", [HID, DM]) for l in range(2)]
    f_b2v = [IN(f"f_b2v{l}", [128, NCH]) for l in range(2)]

    c_ident = IN("c_ident", [128, 128])
    c_neg2id = IN("c_neg2id", [128, 128])
    c_ones_col = IN("c_ones_col", [128, 1])
    c_ones_colf = IN("c_ones_colf", [128, 1])
    c_ones_row_f = IN("c_ones_row_f", [1, 128])
    c_ones33 = IN("c_ones33", [33, 128])
    c_blk2 = IN("c_blk2", [128, 2])
    c_ind2 = IN("c_ind2", [2, 128])
    c_alibi_lhsT = IN("c_alibi_lhsT", [35, 4 * 32 * 128])
    c_alibi_rhs = IN("c_alibi_rhs", [35, 512])
    c_R = IN("c_R", [128, 4 * 512])
    c_qscale = IN("c_qscale", [128, 8])
    c_sqslope = IN("c_sqslope", [128, 8])
    c_rot = {k: IN("c_rot_" + k, [128, T]) for k in ("qc", "qs", "kc", "ks")}
    c_mask_d0 = IN("c_mask_d0", [128, 128])
    c_mask_dm512 = IN("c_mask_dm512", [128, 128])
    c_mask_w0 = IN("c_mask_w0", [128, 128])
    c_gate = IN("c_gate", [128, 1])
    c_base2 = IN("c_base2", [128, 1])

    out_fm = DT("out_fm", [DM, T], F32, kind="ExternalOutput").ap()
    tap_resid = DT("tap_resid", [DM, T], F32, kind="ExternalOutput").ap() if taps else None
    tap_xn = DT("tap_xn", [DM, S], F32, kind="ExternalOutput").ap() if taps else None
    d_spill = DT("d_spill", [DM, T], F32).ap()
    d_snd_col = DT("d_snd_col", [128, NCH], F32)
    d_rcv_col = DT("d_rcv_col", [256, NCH], F32)
    d_snd_kv = DT("d_snd_kv", [1024, 512], F32)
    d_rcv_kv = DT("d_rcv_kv", [2048, 512], F32)

    es = ExitStack()
    E = es.enter_context
    SB = lambda name, shape, dt=BF16: E(nc.sbuf_tensor(name, shape, dt))
    PSUM = lambda name, shape: E(nc.psum_tensor(name, shape, F32))

    s_ident = SB("s_ident", [128, 128])
    s_identf = SB("s_identf", [128, 128], F32)
    s_neg2id = SB("s_neg2id", [128, 128])
    s_ones_col = SB("s_ones_col", [128, 1])
    s_ones_colf = SB("s_ones_colf", [128, 1], F32)
    s_ones_row_f = SB("s_ones_row_f", [1, 128], F32)
    s_ones33 = SB("s_ones33", [33, 128], F32)
    s_blk2 = SB("s_blk2", [128, 2])
    s_ind2 = SB("s_ind2", [2, 128])
    s_al_rhs = SB("s_al_rhs", [35, 512])
    s_qscale = SB("s_qscale", [128, 8], F32)
    s_sqslope = SB("s_sqslope", [128, 8], F32)
    s_gate = SB("s_gate", [128, 1], F32)
    s_base2 = SB("s_base2", [128, 1], F32)
    s_lngb = SB("s_lngb", [128, 2, NCH], F32)
    s_masks = SB("s_masks", [128, 3, 128])
    s_x32 = SB("s_x32", [128, NCH, 512], F32)
    s_xsq = SB("s_xsq", [128, NCH, 512])
    s_rowm = SB("s_rowm", [1, 2048], F32)
    s_rowq = SB("s_rowq", [1, 2048], F32)
    s_pk = SB("s_pk", [128, 64], F32)
    s_tiny = SB("s_tiny", [128, 8], F32)
    s_trow = SB("s_trow", [1, 128], F32)
    s_cbias = SB("s_cbias", [128, 1], F32)
    s_xcol = SB("s_xcol", [128, NCH, 1], F32)

    p_qkA = PSUM("p_qkA", [128, 1024])
    p_qkB = PSUM("p_qkB", [128, 1024])
    p_a = PSUM("p_a", [128, 512])
    p_b = PSUM("p_b", [128, 512])
    p_c = PSUM("p_c", [128, 512])
    p_d = PSUM("p_d", [128, 512])

    sems = {}
    for nm in ("T", "Sc", "V", "CC"):
        sems[nm] = Sem(E(nc.semaphore("sem_" + nm)), nm)
    sT, sS, sV, sCC = (sems[k] for k in ("T", "Sc", "V", "CC"))
    NHS, NGS = 4, 6
    dH_slots = [Sem(E(nc.semaphore(f"sem_dH{i}")), f"dH{i}") for i in range(NHS)]
    dG_slots = [Sem(E(nc.semaphore(f"sem_dG{i}")), f"dG{i}") for i in range(NGS)]
    slot_idx = {"H": 0, "G": 0}
    ALLSEMS = [sT, sS, sV] + dH_slots + dG_slots

    def HALL():
        return [(s, s.n) for s in dH_slots]

    def GALL():
        return [(s, s.n) for s in dG_slots]

    block = E(nc.Block())
    state = {"P": Plan()}

    def P_op(eng, fn, waits=(), incs=(), nd=False):
        state["P"].op(eng, fn, waits, incs, nd)

    def flush():
        state["P"].emit_into(block)
        state["P"] = Plan()

    def barrier():
        for eng in Plan.ENGINES:
            P_op(eng, lambda e: e.nop(), waits=[(s, s.n) for s in ALLSEMS])
        flush()

    cur = lambda s: (s, s.n)

    def dmaH(dst, src, waits=()):
        sl_ = dH_slots[slot_idx["H"] % NHS]
        slot_idx["H"] += 1
        P_op("sync", lambda e, d=dst, s=src: e.dma_start(out=d, in_=s),
             waits=list(waits) + [(sl_, sl_.n)], incs=[(sl_, 16)])
        return (sl_, sl_.n)

    def dmaG(dst, src, waits=(), slow=False):
        sg_ = dG_slots[slot_idx["G"] % NGS]
        slot_idx["G"] += 1
        P_op("gpsimd", lambda e, d=dst, s=src, sl2=slow: e.dma_start(
            out=d, in_=s, allow_slow_non_contiguous=sl2),
             waits=list(waits) + [(sg_, sg_.n)], incs=[(sg_, 16)])
        return (sg_, sg_.n)

    def vec(fn, waits=(), k=1, nd=False):
        P_op("vector", fn, waits=waits, incs=[(sV, k)], nd=nd)
        return sV.n

    def sca(fn, waits=(), k=1, nd=False):
        P_op("scalar", fn, waits=waits, incs=[(sS, k)], nd=nd)
        return sS.n

    def mm(fn, waits=(), k=1):
        P_op("tensor", fn, waits=waits, incs=[(sT, k)])
        return sT.n

    def mm0(fn, waits=()):
        P_op("tensor", fn, waits=waits)

    xfm3 = xfm.rearrange("(c p) t -> p c t", p=128)
    spill3 = d_spill.rearrange("(c p) t -> p c t", p=128)
    out3 = out_fm.rearrange("(c p) t -> p c t", p=128)
    
    dmaG(s_ident[:], c_ident)
    dmaH(s_identf[:], c_ident)
    dmaG(s_neg2id[:], c_neg2id)
    dmaG(s_ones_col[:], c_ones_col)
    dmaH(s_ones_colf[:], c_ones_colf)
    dmaH(s_ones_row_f[:], c_ones_row_f)
    dmaH(s_ones33[:], c_ones33)
    dmaG(s_blk2[:], c_blk2)
    dmaG(s_ind2[:], c_ind2)
    dmaG(s_al_rhs[:], c_alibi_rhs)
    dmaH(s_qscale[:], c_qscale)
    dmaH(s_sqslope[:], c_sqslope)
    dmaH(s_gate[:], c_gate)
    dmaH(s_base2[:], c_base2)
    dmaH(s_lngb[:], ln_gb)
    dmaG(s_masks[:, 0, :], c_mask_d0)
    dmaG(s_masks[:, 1, :], c_mask_dm512)
    dmaG(s_masks[:, 2, :], c_mask_w0)
    barrier()

    def stats_tile(src_fn, dst_m, dst_q, wait_first):
        for c in range(NCH):
            vec(lambda e, c=c: e.tensor_mul(s_xsq[:, c, :], src_fn(c), src_fn(c)),
                waits=(list(wait_first) + [cur(sT), cur(sV)]) if c == 0 else [],
                k=(1 if c == NCH - 1 else 0), nd=True)
        v1 = sV.n
        for c in range(NCH):
            w0 = (list(wait_first) + [cur(sV)]) if c == 0 else []
            mm0(lambda e, c=c: e.matmul(
                p_a[0:1, :], s_ones_col[:], src_fn(c), start=(c == 0),
                stop=(c == NCH - 1), tile_position=(0, 0)), waits=w0)
        for c in range(NCH):
            w0 = [(sV, v1)] if c == 0 else []
            km = [(sT, 1)] if c == NCH - 1 else []
            P_op("tensor", lambda e, c=c: e.matmul(
                p_b[0:1, :], s_ones_col[:], s_xsq[:, c, :], start=(c == 0),
                stop=(c == NCH - 1), tile_position=(0, 0)), waits=w0, incs=km)
        tq = sT.n
        vec(lambda e: e.tensor_copy(dst_m, p_a[0:1, :]), waits=[(sT, tq)],
            nd=True)
        return vec(lambda e: e.tensor_copy(dst_q, p_b[0:1, :]), nd=True)

    def ln_rows(W, double_ln):
        g1a = dmaG(s_pk[:, 0:W], s_rowm[:, 0:128 * W], waits=[cur(sV)])
        g1b = dmaG(s_pk[:, W:2 * W], s_rowq[:, 0:128 * W])

        def cm(e, W=W):
            m = s_pk[:, 0:W]
            q2 = s_pk[:, W:2 * W]
            t1 = s_pk[:, 2 * W:3 * W]
            e.tensor_scalar_mul(m, m, 1.0 / DM)
            e.drain()
            e.tensor_scalar_mul(q2, q2, 1.0 / DM)
            e.drain()
            e.tensor_mul(t1, m, m)
            e.drain()
            e.tensor_sub(q2, q2, t1)
            e.drain()
            return e.tensor_scalar_add(t1, q2, EPS)
        vec(cm, waits=[g1a, g1b], nd=True)
        sca(lambda e, W=W: e.activation(s_pk[:, 3 * W:4 * W],
                                        s_pk[:, 2 * W:3 * W], AF.Sqrt),
            waits=[cur(sV)], nd=True)
        if double_ln:
            def cm2(e, W=W):
                r1 = s_pk[:, 2 * W:3 * W]
                e.reciprocal(r1, s_pk[:, 3 * W:4 * W])
                e.drain()
                t2 = s_pk[:, 3 * W:4 * W]
                e.tensor_mul(t2, r1, r1)
                e.drain()
                e.tensor_mul(t2, s_pk[:, W:2 * W], t2)
                e.drain()
                return e.tensor_scalar_add(t2, t2, EPS)
            vec(cm2, waits=[cur(sS)], nd=True)
            sca(lambda e, W=W: e.activation(s_pk[:, W:2 * W],
                                            s_pk[:, 3 * W:4 * W], AF.Sqrt),
                waits=[cur(sV)], nd=True)

            def cm3(e, W=W):
                r2 = s_pk[:, 3 * W:4 * W]
                e.reciprocal(r2, s_pk[:, W:2 * W])
                e.drain()
                rho = s_pk[:, W:2 * W]
                e.tensor_mul(rho, s_pk[:, 2 * W:3 * W], r2)
                e.drain()
                return e.scalar_tensor_tensor(s_pk[:, 2 * W:3 * W], s_pk[:, 0:W],
                                              -1.0, rho, ALU.mult, ALU.mult)
            vec(cm3, waits=[cur(sS)], nd=True)
        else:
            def cm3b(e, W=W):
                rho = s_pk[:, W:2 * W]
                e.reciprocal(rho, s_pk[:, 3 * W:4 * W])
                e.drain()
                return e.scalar_tensor_tensor(s_pk[:, 2 * W:3 * W], s_pk[:, 0:W],
                                              -1.0, rho, ALU.mult, ALU.mult)
            vec(cm3b, waits=[cur(sS)], nd=True)
        g2a = dmaG(s_rowm[:, 0:128 * W], s_pk[:, W:2 * W], waits=[cur(sV)])
        g2b = dmaG(s_rowq[:, 0:128 * W], s_pk[:, 2 * W:3 * W])
        return [g2a, g2b]

    def norm_accum(src_fn, n_t, idx, first):
        if first:
            vec(lambda e: e.memset(s_tiny[:], 0.0), waits=[cur(sS)])
        for t2 in range(n_t):
            vsq = vec(lambda e, t2=t2: e.tensor_mul(
                s_xsq[:, 0, :], src_fn(t2), src_fn(t2)),
                waits=[cur(sV), cur(sS)] if t2 == 0 else [], nd=True)
            tn = mm(lambda e: e.matmul(p_c[0:2, :], s_blk2[:], s_xsq[:, 0, :],
                                       start=True, stop=True,
                                       tile_position=(0, 0)), waits=[(sV, vsq)])
            vec(lambda e: e.reduce_max(s_tiny[0:2, 2:3], p_c[0:2, :], axis=AX.X),
                waits=[(sT, tn)])
            vec(lambda e, idx=idx: e.tensor_max(
                s_tiny[0:2, idx:idx + 1], s_tiny[0:2, idx:idx + 1],
                s_tiny[0:2, 2:3]))

    def bound_bias(dst, extra_scale):
        sca(lambda e: e.activation(s_tiny[:, 4:6], s_tiny[:, 0:2], AF.Sqrt),
            waits=[cur(sV)])
        sca(lambda e, xs=extra_scale: e.activation(
            s_tiny[:, 4:6], s_tiny[:, 4:6], AF.Copy, scale=xs))
        vec(lambda e: e.tensor_mul(s_tiny[0:2, 6:7], s_tiny[0:2, 4:5],
                                   s_tiny[0:2, 5:6]), waits=[cur(sS)])
        tt_ = mm(lambda e: e.transpose(p_c[0:1, 0:128], s_tiny[:, 6:7], s_identf[:]),
                 waits=[cur(sV)])
        vec(lambda e: e.reduce_max(s_trow[:, 0:1], p_c[0:1, 0:2], axis=AX.X),
            waits=[(sT, tt_)])
        vec(lambda e: e.tensor_scalar(s_trow[:, 1:2], s_trow[:, 0:1], -1.0, -1.0,
                                      ALU.mult, ALU.add))
        tc = mm(lambda e: e.matmul(p_c[:, 0:1], s_ones_row_f[:], s_trow[:, 1:2],
                                   start=True, stop=True, tile_position=(0, 0)),
                waits=[cur(sV)])
        return vec(lambda e, dst=dst: e.tensor_copy(dst, p_c[:, 0:1]),
                   waits=[(sT, tc)])

    ab = ExitStack()
    A = ab.enter_context
    s_xn = A(nc.sbuf_tensor("s_xn", [128, NCH, S], BF16))
    s_U = A(nc.sbuf_tensor("s_U", [128, NCH, T], BF16))
    s_K = A(nc.sbuf_tensor("s_K", [128, S], BF16))
    s_Q = A(nc.sbuf_tensor("s_Q", [128, T], BF16))
    s_Vt = A(nc.sbuf_tensor("s_Vt", [128, 32, 128], BF16))
    s_Vf = A(nc.sbuf_tensor("s_Vf", [128, 2, 512], F32))
    s_eS = A(nc.sbuf_tensor("s_eS", [128, 2, 2, 512], BF16))
    s_al_lhsT = A(nc.sbuf_tensor("s_al_lhsT", [35, 32, 128], BF16))
    s_R = A(nc.sbuf_tensor("s_R", [128, 4, 512], BF16))
    s_wq = A(nc.sbuf_tensor("s_wq", [128, NCH, 128], BF16))
    s_wk = A(nc.sbuf_tensor("s_wk", [128, NCH, 128], BF16))
    s_wv = A(nc.sbuf_tensor("s_wv", [128, NCH, 128], BF16))
    s_wo2 = A(nc.sbuf_tensor("s_wo2", [128, 2, NCH, 128], BF16))
    s_zw = A(nc.sbuf_tensor("s_zw", [64, 2048], F32))
    s_zacc = A(nc.sbuf_tensor("s_zacc", [128, 4, 512], F32))
    s_zpk = A(nc.sbuf_tensor("s_zpk", [128, 32], F32))

    dmaG(s_R[:], c_R.rearrange("p (d n) -> p d n", d=4))

    for half_i in range(2):
        base = 2048 * half_i
        for tt in range(4):
            sl = slice(base + 512 * tt, base + 512 * (tt + 1))
            gB = dmaG(s_xn[:, :, sl], xfm3[:, :, sl], waits=[cur(sT)])
            stats_tile(lambda c, sl=sl: s_xn[:, c, sl],
                       s_rowm[:, ts(tt, 512)], s_rowq[:, ts(tt, 512)],
                       [gB])
        g2 = ln_rows(16, double_ln=True)
        for tt in range(4):
            sl = slice(base + 512 * tt, base + 512 * (tt + 1))
            hA = dmaH(s_x32[:], xfm3[:, :, sl], waits=[cur(sV), *HALL()])
            mm(lambda e, tt=tt: e.matmul(p_c[:], s_ones_row_f[:],
                                         s_rowm[:, ts(tt, 512)], start=True,
                                         stop=True, tile_position=(0, 0)),
               waits=list(g2) + [cur(sV)])
            tB = mm(lambda e, tt=tt: e.matmul(p_d[:], s_ones_row_f[:],
                                              s_rowq[:, ts(tt, 512)], start=True,
                                              stop=True, tile_position=(0, 0)))
            for c in range(NCH):
                w0 = [(sT, tB), hA] if c == 0 else []
                vec(lambda e, c=c: e.tensor_mul(s_x32[:, c, :], s_x32[:, c, :],
                                                p_c[:]), waits=w0, k=0)
                vec(lambda e, c=c: e.tensor_add(s_x32[:, c, :], s_x32[:, c, :],
                                                p_d[:]), k=0)
                vec(lambda e, c=c, sl=sl: e.tensor_scalar(
                    s_xn[:, c, sl], s_x32[:, c, :],
                    s_lngb[:, 0, c:c + 1], s_lngb[:, 1, c:c + 1],
                    ALU.mult, ALU.add), k=(1 if c == NCH - 1 else 0))
        if taps:
            for tt in range(4):
                sl = slice(base + 512 * tt, base + 512 * (tt + 1))
                dmaH(tap_xn.rearrange("(c p) t -> p c t", p=128)[:, :, sl],
                     s_xn[:, :, sl], waits=[cur(sV)])
    flush()

    proj_mark = {"v": 0}
    for hp in range(8):
        ha, hb = 2 * hp, 2 * hp + 1
        colsl = ts(hp, 128)
        wload = [(sT, proj_mark["v"])] if proj_mark["v"] else [cur(sT)]
        gw = [dmaG(s_wq[:], Wq.rearrange("(c p) i -> p c i", p=128)[:, :, colsl],
                   waits=wload),
              dmaG(s_wk[:], Wk.rearrange("(c p) i -> p c i", p=128)[:, :, colsl]),
              dmaG(s_wv[:], Wv.rearrange("(c p) i -> p c i", p=128)[:, :, colsl])]
        q_ev = {}
        for tt in range(4):
            sl = ts(tt, 512)
            pq = p_a if tt % 2 == 0 else p_b
            for c in range(NCH):
                if c == 0:
                    w0 = ([(sS, q_ev[tt - 2])] if tt >= 2
                          else gw + [cur(sV), cur(sS)])
                else:
                    w0 = []
                km = [(sT, 1)] if c == NCH - 1 else []
                P_op("tensor", lambda e, c=c, sl=sl, pq=pq: e.matmul(
                    pq[:], s_wq[:, c, :], s_xn[:, c, sl], start=(c == 0),
                    stop=(c == NCH - 1), tile_position=(0, 0)), waits=w0, incs=km)
            tq = sT.n
            q_ev[tt] = sca(lambda e, sl=sl, pq=pq, hp=hp: e.activation(
                s_Q[:, sl], pq[:], AF.Copy, scale=s_qscale[:, hp:hp + 1]),
                waits=[(sT, tq)], nd=True)
        k_ev = {}
        v_ev = {}
        vt_cp = {}
        pcs = [p_c, p_qkA]
        pds = [p_d, p_qkB]

        def emit_vt(tt):
            for j in range(4):
                idx = 4 * tt + j
                pdx = pds[idx % 2]
                wt = [(sV, v_ev[tt])] if j == 0 else []
                if idx >= 2:
                    wt.append((sV, vt_cp[idx - 2]))
                tj = mm(lambda e, j=j, tt=tt, pdx=pdx: e.transpose(
                    pdx[:, 0:128], s_Vf[:, tt % 2, ts(j, 128)], s_identf[:]),
                    waits=wt)
                vt_cp[idx] = vec(lambda e, idx=idx, pdx=pdx:
                                 e.tensor_copy(s_Vt[:, idx, :],
                                               pdx[:, 0:128]),
                                 waits=[(sT, tj)], nd=True)

        for tt in range(8):
            sl = ts(tt, 512)
            pk_ = p_a if tt % 2 == 0 else p_b
            for c in range(NCH):
                if c == 0:
                    w0 = ([(sV, k_ev[tt - 2])] if tt >= 2
                          else [(sS, q_ev[tt + 2])])
                else:
                    w0 = []
                km = [(sT, 1)] if c == NCH - 1 else []
                P_op("tensor", lambda e, c=c, sl=sl, pk_=pk_: e.matmul(
                    pk_[:], s_wk[:, c, :], s_xn[:, c, sl], start=(c == 0),
                    stop=(c == NCH - 1), tile_position=(0, 0)), waits=w0, incs=km)
            tk = sT.n
            k_ev[tt] = vec(lambda e, sl=sl, pk_=pk_: e.tensor_copy(
                s_K[:, sl], pk_[:]), waits=[(sT, tk)], nd=True)
            pv_ = pcs[tt % 2]
            for c in range(NCH):
                if c == 0:
                    w0 = ([(sV, v_ev[tt - 2])] if tt >= 2 else [cur(sV)])
                else:
                    w0 = []
                km = [(sT, 1)] if c == NCH - 1 else []
                P_op("tensor", lambda e, c=c, sl=sl, pv_=pv_: e.matmul(
                    pv_[:, 0:512], s_wv[:, c, :], s_xn[:, c, sl], start=(c == 0),
                    stop=(c == NCH - 1), tile_position=(0, 0)), waits=w0, incs=km)
            tv = sT.n
            v_ev[tt] = vec(lambda e, tt=tt, pv_=pv_: e.tensor_copy(
                s_Vf[:, tt % 2, :], pv_[:, 0:512]), waits=[(sT, tv)], nd=True)
            if tt >= 1:
                emit_vt(tt - 1)
        emit_vt(7)

        proj_mark["v"] = sT.n
        norm_accum(lambda t2: s_Q[:, ts(t2, 512)], 4, 0, True)
        norm_accum(lambda t2: s_K[:, ts(t2, 512)], 8, 1, False)
        cb_v = bound_bias(s_cbias[:], s_sqslope[:, hp:hp + 1])
        zmm_prev = [0]

        for qt in range(4):
            qsl = ts(qt, 512)
            ga = dmaG(s_al_lhsT[:],
                      c_alibi_lhsT.rearrange("p (v k m) -> p v k m", v=4, k=32)
                      [:, qt, :, :], waits=[cur(sT)])
            exp_marks = {}
            zpv_marks = {}
            za_marks = {}

            def emit_zpv(kc):
                ring = kc % 2
                em = exp_marks[kc]
                zw = [(sS, em)]
                if kc == 0:
                    zw.append(cur(sV))
                mm0(lambda e, ring=ring, kc=kc: e.matmul(
                    p_a[0:64, :], s_Vt[:, kc, 0:64], s_eS[:, 0, ring, :],
                    start=(kc == 0), stop=(kc == 31), tile_position=(0, 0)),
                    waits=zw)
                P_op("tensor", lambda e, ring=ring, kc=kc: e.matmul(
                    p_b[64:128, :], s_Vt[:, kc, 64:128], s_eS[:, 1, ring, :],
                    start=(kc == 0), stop=(kc == 31), tile_position=(0, 64)),
                    incs=[(sT, 1)])
                zpv_marks[kc] = sT.n
                vw = [(sS, em)]
                if kc < 2:
                    vw.append((sT, zmm_prev[0]))
                sa = 2 * (kc % 2)
                sb = 2 * (kc % 2) + 1
                if kc < 2:
                    vec(lambda e, ring=ring, sa=sa: e.tensor_copy(
                        s_zacc[:, sa, :], s_eS[:, 0, ring, :]), waits=vw,
                        k=0, nd=True)
                    za_marks[kc] = vec(lambda e, ring=ring, sb=sb: e.tensor_copy(
                        s_zacc[:, sb, :], s_eS[:, 1, ring, :]), nd=True)
                else:
                    vw.append((sV, za_marks[kc - 2]))
                    vec(lambda e, ring=ring, sa=sa: e.tensor_add(
                        s_zacc[:, sa, :], s_zacc[:, sa, :], s_eS[:, 0, ring, :]),
                        waits=vw, k=0, nd=True)
                    za_marks[kc] = vec(lambda e, ring=ring, sb=sb: e.tensor_add(
                        s_zacc[:, sb, :], s_zacc[:, sb, :], s_eS[:, 1, ring, :]),
                        nd=True)

            for kc in range(32):
                ksl = ts(kc, 128)
                half = ts(kc % 2, 512)
                w0 = []
                if kc == 0:
                    w0 = [ga, (sV, cb_v), cur(sS), cur(sV), cur(sT)]
                elif kc >= 2:
                    w0 = [(sS, exp_marks[kc - 2])]
                mm0(lambda e, ksl=ksl, qsl=qsl, half=half: e.matmul(
                    p_qkA[:, half], s_K[0:64, ksl], s_Q[0:64, qsl],
                    start=True, stop=False, tile_position=(0, 0)), waits=w0)
                mm0(lambda e, ksl=ksl, qsl=qsl, half=half: e.matmul(
                    p_qkB[:, half], s_K[64:128, ksl], s_Q[64:128, qsl],
                    start=True, stop=False, tile_position=(64, 0)))
                straddle = (4 * qt <= kc <= 4 * qt + 3)
                mm0(lambda e, kc=kc, half=half, st=straddle: e.matmul(
                    p_qkA[:, half], s_al_lhsT[0:3, kc, :], s_al_rhs[0:3, :],
                    start=False, stop=not st, tile_position=(0, 0)))
                km = [] if straddle else [(sT, 1)]
                P_op("tensor", lambda e, kc=kc, half=half, st=straddle: e.matmul(
                    p_qkB[:, half], s_al_lhsT[32:35, kc, :], s_al_rhs[32:35, :],
                    start=False, stop=not st, tile_position=(32, 0)), incs=km)
                if straddle:
                    di = kc - 4 * qt
                    mm0(lambda e, di=di, half=half: e.matmul(
                        p_qkA[:, half], s_neg2id[:], s_R[:, di, :],
                        start=False, stop=True, tile_position=(0, 0)))
                    P_op("tensor", lambda e, di=di, half=half: e.matmul(
                        p_qkB[:, half], s_neg2id[:], s_R[:, di, :],
                        start=False, stop=True, tile_position=(0, 0)),
                        incs=[(sT, 1)])
                tqk = sT.n
                ring = kc % 2
                we = [(sT, tqk)]
                if kc >= 2:
                    we.append((sT, zpv_marks[kc - 2]))
                    we.append((sV, za_marks[kc - 2]))
                sca(lambda e, ring=ring, half=half, scA=float(SLOPES[ha]):
                    e.activation(
                    s_eS[:, 0, ring, :], p_qkA[:, half], AF.Exp,
                    bias=s_cbias[:], scale=scA), waits=we, nd=True)
                em = sca(lambda e, ring=ring, half=half, scB=float(SLOPES[hb]):
                    e.activation(
                    s_eS[:, 1, ring, :], p_qkB[:, half], AF.Exp,
                    bias=s_cbias[:], scale=scB), nd=True)
                exp_marks[kc] = em
                if kc >= 1:
                    emit_zpv(kc - 1)
            emit_zpv(31)
            mm(lambda e: e.matmul(p_c[0:1, :], s_ones_colf[:], s_zacc[:, 0, :],
                                  start=True, stop=False, tile_position=(0, 0)),
               waits=[(sV, za_marks[31])])
            mm(lambda e: e.matmul(p_c[0:1, :], s_ones_colf[:], s_zacc[:, 2, :],
                                  start=False, stop=True, tile_position=(0, 0)))
            mm(lambda e: e.matmul(p_d[32:33, :], s_ones_colf[:],
                                  s_zacc[:, 1, :], start=True, stop=False,
                                  tile_position=(0, 32)))
            mm(lambda e: e.matmul(p_d[32:33, :], s_ones_colf[:],
                                  s_zacc[:, 3, :], start=False, stop=True,
                                  tile_position=(0, 32)))
            zmm_prev[0] = sT.n
            tz = sT.n
            vec(lambda e, qt=qt: e.tensor_copy(
                s_zw[0:1, 512 * qt:512 * qt + 512], p_c[0:1, :]),
                waits=[(sT, tz)], nd=True)
            vec(lambda e, qt=qt: e.tensor_copy(
                s_zw[32:33, 512 * qt:512 * qt + 512], p_d[32:33, :]), nd=True)
            vec(lambda e, qsl=qsl, hp=hp: e.tensor_copy(s_U[0:64, hp, qsl],
                                                        p_a[0:64, :]), nd=True)
            vec(lambda e, qsl=qsl, hp=hp: e.tensor_copy(s_U[64:128, hp, qsl],
                                                        p_b[64:128, :]),
                nd=True)
        gza = dmaG(s_zpk[:, 0:16], s_zw[0:1, :], waits=[cur(sV)])
        gzb = dmaG(s_zpk[:, 16:32], s_zw[32:33, :])
        vz = vec(lambda e: e.reciprocal(s_zpk[:], s_zpk[:]), waits=[gza, gzb])
        gz2a = dmaG(s_zw[0:1, :], s_zpk[:, 0:16], waits=[(sV, vz)])
        gz2b = dmaG(s_zw[32:33, :], s_zpk[:, 16:32])
        for qt in range(4):
            qsl = ts(qt, 512)
            tba = mm(lambda e, qt=qt: e.matmul(
                p_c[0:64, :], s_ones_row_f[0:1, 0:64],
                s_zw[0:1, 512 * qt:512 * qt + 512], start=True, stop=True,
                tile_position=(0, 0)), waits=[gz2a, gz2b, cur(sV)])
            tbb = mm(lambda e, qt=qt: e.matmul(
                p_d[64:128, :], s_ones33[32:33, 0:64],
                s_zw[32:33, 512 * qt:512 * qt + 512], start=True, stop=True,
                tile_position=(32, 64)))
            vec(lambda e, hp=hp, qsl=qsl: e.tensor_mul(
                s_U[0:64, hp, qsl], s_U[0:64, hp, qsl], p_c[0:64, :]),
                waits=[(sT, tbb)] + ([cur(sV)] if qt == 0 else []), nd=True)
            vec(lambda e, hp=hp, qsl=qsl: e.tensor_mul(
                s_U[64:128, hp, qsl], s_U[64:128, hp, qsl], p_d[64:128, :]),
                nd=True)
    flush()

    for tt in range(4):
        sl = ts(tt, 512)
        hx = dmaH(s_x32[:], xfm3[:, :, sl], waits=[cur(sV), *HALL()])
        for d in range(NCH):
            pd_ = p_a if d % 2 == 0 else p_b
            gwo = dmaG(s_wo2[:, d % 2, :, :],
                       Wo.rearrange("(c p) d -> p c d", p=128)[:, :, ts(d, 128)],
                       waits=[cur(sT)])
            for ic in range(NCH):
                w0 = [gwo, cur(sV)] if ic == 0 else []
                km = [(sT, 1)] if ic == NCH - 1 else []
                P_op("tensor", lambda e, d=d, ic=ic, sl=sl, pd_=pd_: e.matmul(
                    pd_[:], s_wo2[:, d % 2, ic, :], s_U[:, ic, sl],
                    start=(ic == 0), stop=(ic == NCH - 1),
                    tile_position=(0, 0)), waits=w0, incs=km)
            tw = sT.n
            vec(lambda e, d=d, pd_=pd_: e.tensor_add(
                s_x32[:, d, :], s_x32[:, d, :], pd_[:]),
                waits=[(sT, tw), hx], nd=True)
        vr = sV.n
        dmaH(spill3[:, :, sl], s_x32[:], waits=[(sV, vr)])
        if taps:
            dmaH(tap_resid.rearrange("(c p) t -> p c t", p=128)[:, :, sl],
                 s_x32[:], waits=[(sV, vr)])
    flush()
    ab.close()
    barrier()

    if n_ffn >= 1:
        fd = ExitStack()
        FA = fd.enter_context
        s_xf = FA(nc.sbuf_tensor("s_xf", [128, NCH, 1 + T], F32))
        s_fqs = FA(nc.sbuf_tensor("s_fqs", [128, 2], F32))
        s_lgb2 = FA(nc.sbuf_tensor("s_lgb2", [128, 2, NCH], F32))
        s_b2f = FA(nc.sbuf_tensor("s_b2f", [128, NCH], F32))
        s_b1 = FA(nc.sbuf_tensor("s_b1", [128, 2, NAB], F32))
        s_fcb = FA(nc.sbuf_tensor("s_fcb", [128, 1], F32))

        def exch_col():
            g0 = dmaG(d_snd_col.ap(), s_xf[:, :, T:T + 1], waits=[cur(sV)], slow=True)
            P_op("gpsimd", lambda e: e.collective_compute(
                "AllGather", ALU.bypass,
                replica_groups=[[0, 1], [2, 3], [4, 5], [6, 7]],
                ins=[d_snd_col.ap().opt()], outs=[d_rcv_col.ap().opt()]),
                waits=[g0], incs=[(sCC, 1)])
            cc = sCC.n
            gr = dmaG(s_xcol[:], d_rcv_col.ap()[0:128, :], waits=[(sCC, cc)], slow=True)
            for c in range(NCH):
                vec(lambda e, c=c: e.tensor_mul(s_xf[:, c, 0:1], s_xcol[:, c, :],
                                                s_gate[:]),
                    waits=[gr] if c == 0 else [],
                    k=(1 if c == NCH - 1 else 0))
            return sV.n

        cs = ExitStack()
        s_xtc = cs.enter_context(nc.sbuf_tensor("s_xtc", [128, NCH, 512], BF16))
        for tt in range(4):
            sl = ts(tt, 512)
            gB = dmaG(s_xtc[:], spill3[:, :, sl],
                      waits=[cur(sV), cur(sT), *HALL()])
            stats_tile(lambda c: s_xtc[:, c, :],
                       s_rowm[:, ts(tt, 512)], s_rowq[:, ts(tt, 512)], [gB])
        g2 = ln_rows(16, double_ln=False)
        for tt in range(4):
            sl = ts(tt, 512)
            hA = dmaH(s_x32[:], spill3[:, :, sl], waits=[cur(sV), *HALL()])
            mm(lambda e, tt=tt: e.matmul(p_c[:], s_ones_row_f[:],
                                         s_rowm[:, ts(tt, 512)], start=True,
                                         stop=True, tile_position=(0, 0)),
               waits=list(g2) + [cur(sV)])
            tB = mm(lambda e, tt=tt: e.matmul(p_d[:], s_ones_row_f[:],
                                              s_rowq[:, ts(tt, 512)], start=True,
                                              stop=True, tile_position=(0, 0)))
            for c in range(NCH):
                w0 = [(sT, tB), hA] if c == 0 else []
                vec(lambda e, c=c: e.tensor_mul(s_x32[:, c, :], s_x32[:, c, :],
                                                p_c[:]), waits=w0, k=0)
                vec(lambda e, c=c, tt=tt: e.tensor_add(
                    s_xf[:, c, 1 + 512 * tt:1 + 512 * (tt + 1)],
                    s_x32[:, c, :], p_d[:]), k=(1 if c == NCH - 1 else 0))
        flush()
        cs.close()
        barrier()
        exch_col()
        flush()

        for l in range(n_ffn):
            Ls = ExitStack()
            s_fq = Ls.enter_context(nc.sbuf_tensor(f"s_fq{l}", [128, 4, T], BF16))
            s_fk = Ls.enter_context(
                nc.sbuf_tensor(f"s_fk{l}", [128, 4, WIN + T], BF16))
            s_fk2 = Ls.enter_context(
                nc.sbuf_tensor(f"s_fk2{l}", [128, 512], BF16))
            s_fv = Ls.enter_context(
                nc.sbuf_tensor(f"s_fv{l}", [128, 20, 512], BF16))

            vmarks = [dmaH(s_fqs[:], f_qsv[l], waits=[cur(sV)]),
                      dmaH(s_lgb2[:], f_lgb[l]),
                      dmaH(s_b2f[:], f_b2v[l]),
                      dmaH(s_b1[:], f_b1v[l])]
            vec(lambda e: e.nop(), waits=vmarks)
            sca(lambda e: e.nop(), waits=vmarks)

            sc1 = ExitStack()
            s_xt = sc1.enter_context(
                nc.sbuf_tensor(f"s_xt{l}", [128, NCH, 513], BF16))
            s_fwq = sc1.enter_context(
                nc.sbuf_tensor(f"s_fwq{l}", [128, NCH, 1024], BF16))

            gwq = dmaG(s_fwq[:], fw_qkv[l].rearrange(
                "(c p) i -> p c i", p=128)[:, :, 0:1024], waits=[cur(sT)])

            def xt_sl(c, t0, n):
                off = 0 if c >= 4 else 1
                return s_xt[:, c, off + t0: off + t0 + n]

            for tt in range(4):
                sl = ts(tt, 512)
                gxt = dmaG(s_xt[:], s_xf[:, :, 512 * tt:512 * tt + 513],
                           waits=[cur(sV), cur(sT)])
                for oc in range(8):
                    po = p_a if oc % 2 == 0 else p_b
                    for c in range(NCH):
                        w0 = [gwq, gxt, cur(sV)] if c == 0 else []
                        km = [(sT, 1)] if c == NCH - 1 else []
                        P_op("tensor", lambda e, oc=oc, c=c, po=po: e.matmul(
                            po[:], s_fwq[:, c, ts(oc, 128)], xt_sl(c, 0, 512),
                            start=(c == 0), stop=(c == NCH - 1),
                            tile_position=(0, 0)), waits=w0, incs=km)
                    tp = sT.n
                    if oc < 4:
                        vec(lambda e, oc=oc, sl=sl, po=po: e.tensor_copy(
                            s_fq[:, oc, sl], po[:]), waits=[(sT, tp)], nd=True)
                    else:
                        vec(lambda e, oc=oc, tt=tt, po=po: e.tensor_copy(
                            s_fk[:, oc - 4, WIN + 512 * tt:WIN + 512 * (tt + 1)],
                            po[:]), waits=[(sT, tp)], nd=True)
            gwv = dmaG(s_fwq[:, :, 0:512], fw_qkv[l].rearrange(
                "(c p) i -> p c i", p=128)[:, :, 1024:1536], waits=[cur(sT)])
            for tt in range(4):
                gxt = dmaG(s_xt[:], s_xf[:, :, 512 * tt:512 * tt + 513],
                           waits=[cur(sV), cur(sT)])
                for j in range(4):
                    tchunk = 4 * tt + j
                    for c in range(NCH):
                        w0 = [gwv, gxt, cur(sV)] if c == 0 else []
                        km = [(sT, 1)] if c == NCH - 1 else []
                        P_op("tensor", lambda e, c=c, j=j: e.matmul(
                            p_c[:], xt_sl(c, 128 * j, 128), s_fwq[:, c, 0:512],
                            start=(c == 0), stop=(c == NCH - 1),
                            tile_position=(0, 0)), waits=w0, incs=km)
                    tvv = sT.n
                    vec(lambda e, tchunk=tchunk: e.tensor_copy(
                        s_fv[:, 4 + tchunk, :], p_c[:]), waits=[(sT, tvv)],
                        nd=True)
            flush()
            sc1.close()
            barrier()

            sc2 = ExitStack()
            s_rot = {k: sc2.enter_context(
                nc.sbuf_tensor(f"s_rot_{k}{l}", [128, T], BF16))
                for k in ("qc", "qs", "kc", "ks")}
            s_rtmp = sc2.enter_context(nc.sbuf_tensor(f"s_rtmp{l}", [128, T], BF16))
            s_ttmp = sc2.enter_context(nc.sbuf_tensor(f"s_ttmp{l}", [128, T], BF16))
            s_rn = sc2.enter_context(nc.sbuf_tensor(f"s_rn{l}", [2, 2, 512], BF16))
            for k in ("qc", "qs", "kc", "ks"):
                dmaG(s_rot[k][:], c_rot[k])
            for pr in range(4):
                for tt in range(4):
                    sl = slice(512 * tt, 512 * tt + 512)
                    vsq = vec(lambda e, pr=pr, sl=sl: e.tensor_mul(
                        s_xsq[:, 0, :], s_fq[:, pr, sl], s_fq[:, pr, sl]))
                    tn = mm(lambda e: e.matmul(p_c[0:2, :], s_blk2[:],
                                               s_xsq[:, 0, :], start=True,
                                               stop=True, tile_position=(0, 0)),
                            waits=[(sV, vsq), cur(sS)])
                    vq = vec(lambda e: e.tensor_scalar_add(
                        p_c[0:2, :], p_c[0:2, :], 1e-24), waits=[(sT, tn)])
                    sln = sca(lambda e: e.activation(
                        p_c[0:2, :], p_c[0:2, :], AF.Ln),
                        waits=[(sV, vq)], nd=True)
                    sca(lambda e: e.activation(
                        s_rn[:, 0, :], p_c[0:2, :], AF.Exp, scale=-0.5),
                        waits=[(sS, sln)], nd=True)
                    vsq = vec(lambda e, pr=pr, sl=sl: e.tensor_mul(
                        s_xsq[:, 1, :], s_fk[:, pr, WIN + sl.start:WIN + sl.stop],
                        s_fk[:, pr, WIN + sl.start:WIN + sl.stop]))
                    tn = mm(lambda e: e.matmul(p_d[0:2, :], s_blk2[:],
                                               s_xsq[:, 1, :], start=True,
                                               stop=True, tile_position=(0, 0)),
                            waits=[(sV, vsq), cur(sS)])
                    vq = vec(lambda e: e.tensor_scalar_add(
                        p_d[0:2, :], p_d[0:2, :], 1e-24), waits=[(sT, tn)])
                    sln2 = sca(lambda e: e.activation(
                        p_d[0:2, :], p_d[0:2, :], AF.Ln),
                        waits=[(sV, vq)], nd=True)
                    smn = sca(lambda e: e.activation(
                        s_rn[:, 1, :], p_d[0:2, :], AF.Exp, scale=-0.5),
                        waits=[(sS, sln2)], nd=True)
                    tb = mm(lambda e: e.matmul(
                        p_c[:], s_ind2[:], s_rn[:, 0, :], start=True, stop=True,
                        tile_position=(0, 0)), waits=[(sS, smn), cur(sV)])
                    vec(lambda e, pr=pr, sl=sl: e.scalar_tensor_tensor(
                        s_fq[:, pr, sl], s_fq[:, pr, sl], s_fqs[:, 0:1], p_c[:],
                        ALU.mult, ALU.mult), waits=[(sT, tb)], nd=True)
                    tb2 = mm(lambda e: e.matmul(
                        p_d[:], s_ind2[:], s_rn[:, 1, :], start=True, stop=True,
                        tile_position=(0, 0)), waits=[cur(sV)])
                    vec(lambda e, pr=pr, sl=sl: e.scalar_tensor_tensor(
                        s_fk[:, pr, WIN + sl.start:WIN + sl.stop],
                        s_fk[:, pr, WIN + sl.start:WIN + sl.stop],
                        s_fqs[:, 1:2], p_d[:], ALU.mult, ALU.mult),
                        waits=[(sT, tb2)], nd=True)
                vnorm = sV.n
                swaps = ((0, 32), (32, 0), (64, 96), (96, 64))
                gq = []
                for dsts, srcs in swaps:
                    gq.append(dmaG(s_rtmp[dsts:dsts + 32, :],
                                   s_fq[srcs:srcs + 32, pr, :],
                                   waits=[(sV, vnorm)] if dsts == 0 else []))
                vec(lambda e, pr=pr: e.tensor_mul(s_ttmp[:], s_fq[:, pr, :],
                                                  s_rot["qc"][:]),
                    waits=gq, nd=True)
                m2_ = vec(lambda e: e.tensor_mul(s_rtmp[:], s_rtmp[:],
                                                 s_rot["qs"][:]), nd=True)
                vec(lambda e, pr=pr: e.tensor_add(s_fq[:, pr, :], s_ttmp[:],
                                                  s_rtmp[:]),
                    waits=[(sV, m2_)], nd=True)
                gk = []
                for dsts, srcs in swaps:
                    gk.append(dmaG(s_rtmp[dsts:dsts + 32, :],
                                   s_fk[srcs:srcs + 32, pr, WIN:WIN + T],
                                   waits=[(sV, sV.n)] if dsts == 0 else []))
                vec(lambda e, pr=pr: e.tensor_mul(
                    s_ttmp[:], s_fk[:, pr, WIN:WIN + T], s_rot["kc"][:]),
                    waits=gk, nd=True)
                m4_ = vec(lambda e: e.tensor_mul(s_rtmp[:], s_rtmp[:],
                                                 s_rot["ks"][:]), nd=True)
                vec(lambda e, pr=pr: e.tensor_add(s_fk[:, pr, WIN:WIN + T],
                                                  s_ttmp[:], s_rtmp[:]),
                    waits=[(sV, m4_)], nd=True)
            flush()
            sc2.close()
            barrier()

            vkv = sV.n
            snd_marks = []
            for p in range(4):
                snd_marks.append(dmaG(d_snd_kv.ap()[ts(p, 128), :],
                                      s_fk[:, p, WIN + T - 512:WIN + T],
                                      waits=[(sV, vkv)] if p == 0 else []))
            for p in range(4):
                snd_marks.append(dmaG(
                    d_snd_kv.ap()[512 + 128 * p:512 + 128 * (p + 1), :],
                    s_fv[:, 16 + p, :]))
            P_op("gpsimd", lambda e: e.collective_compute(
                "AllGather", ALU.bypass,
                replica_groups=[[0, 1], [2, 3], [4, 5], [6, 7]],
                ins=[d_snd_kv.ap().opt()], outs=[d_rcv_kv.ap().opt()]),
                waits=snd_marks, incs=[(sCC, 1)])
            ccm = sCC.n
            hal_marks = []
            for p in range(4):
                hal_marks.append(dmaG(s_fk[:, p, 0:WIN],
                                      d_rcv_kv.ap()[ts(p, 128), :],
                                      waits=[(sCC, ccm)] if p == 0 else []))
            for p in range(4):
                hal_marks.append(dmaG(
                    s_fv[:, p, :],
                    d_rcv_kv.ap()[512 + 128 * p:512 + 128 * (p + 1), :]))
            ghal = hal_marks
            flush()

            sc3 = ExitStack()
            s_feS = sc3.enter_context(
                nc.sbuf_tensor(f"s_feS{l}", [128, 4, 640], BF16))
            s_fU = sc3.enter_context(
                nc.sbuf_tensor(f"s_fU{l}", [128, 4, T], BF16))
            s_fwo = sc3.enter_context(
                nc.sbuf_tensor(f"s_fwo{l}", [128, 2, DM], BF16))
            s_zwf = sc3.enter_context(nc.sbuf_tensor(f"s_zwf{l}", [64, 2048], F32))
            s_zaccf = sc3.enter_context(
                nc.sbuf_tensor(f"s_zaccf{l}", [128, 4, 128], F32))
            s_zpk2 = sc3.enter_context(nc.sbuf_tensor(f"s_zpk2{l}", [128, 32], F32))
            vec(lambda e: e.nop(), waits=list(ghal))
            tq_ = mm(lambda e: e.transpose(p_c[0:2, 0:128], s_fqs[:], s_identf[:]),
                     waits=[cur(sS), cur(sV)])
            vec(lambda e: e.reduce_max(s_tiny[0:2, 0:1], p_c[0:2, 0:128],
                                       axis=AX.X, apply_absolute_value=True),
                waits=[(sT, tq_)])
            tp_ = mm(lambda e: e.matmul(p_c[0:1, 0:1], s_tiny[0:2, 0:1],
                                        s_tiny[0:2, 0:1], start=True, stop=True,
                                        tile_position=(0, 0)),
                     waits=[cur(sV)])
            vec(lambda e: e.tensor_scalar(s_trow[:, 1:2], p_c[0:1, 0:1], -4.0,
                                          -1.0, ALU.mult, ALU.add),
                waits=[(sT, tp_)])
            tb_ = mm(lambda e: e.matmul(p_c[:, 0:1], s_ones_row_f[:],
                                        s_trow[:, 1:2], start=True, stop=True,
                                        tile_position=(0, 0)), waits=[cur(sV)])
            fcb_v = vec(lambda e: e.tensor_copy(s_fcb[:], p_c[:, 0:1]),
                        waits=[(sT, tb_)])

            for pr in range(4):
                for w in range(4):
                    k2v = vec(lambda e, pr=pr, w=w: e.tensor_scalar_mul(
                        s_fk2[:], s_fk[:, pr, 512 * w:512 * w + 512],
                        s_base2[:]), waits=list(ghal) + [cur(sT)], nd=True)
                    exp_m = {}
                    zpv_m = {}
                    cmb_m = {}
                    zmmu = [0]

                    def emit_zpv_f(u, pr=pr, w=w):
                        em = exp_m[u]
                        slA = 2 * (u % 2)
                        slB = 2 * (u % 2) + 1
                        zam = {}
                        for kr in range(5):
                            kc = u + kr
                            esl = slice(128 * kr, 128 * kr + 128)
                            gstart = (u == 0 and kr == 0)
                            gstop = (u == 3 and kr == 4)
                            zw = [(sS, em)] if kr == 0 else []
                            if u == 0 and kr == 0:
                                zw.append(cur(sV))
                            vch = 4 * w + kc
                            mm0(lambda e, esl=esl, slA=slA, u=u, vch=vch, pr=pr,
                                gstart=gstart, gstop=gstop: e.matmul(
                                    p_a[0:64, 128 * u:128 * u + 128],
                                    s_fv[:, vch, 128 * pr:128 * pr + 64],
                                    s_feS[:, slA, esl],
                                    start=gstart, stop=gstop,
                                    tile_position=(0, 0)), waits=zw)
                            km = [(sT, 1)] if kr == 4 else []
                            P_op("tensor", lambda e, esl=esl, slB=slB, u=u,
                                 vch=vch, pr=pr, gstart=gstart, gstop=gstop:
                                 e.matmul(
                                     p_b[64:128, 128 * u:128 * u + 128],
                                     s_fv[:, vch, 128 * pr + 64:128 * pr + 128],
                                     s_feS[:, slB, esl],
                                     start=gstart, stop=gstop,
                                     tile_position=(0, 64)), incs=km)
                            sa = 2 * (kr % 2)
                            sb = sa + 1
                            vw = [(sS, em)] if kr == 0 else []
                            if kr < 2:
                                vw.append((sT, zmmu[0]))
                                vec(lambda e, esl=esl, slA=slA, sa=sa:
                                    e.tensor_copy(s_zaccf[:, sa, :],
                                                  s_feS[:, slA, esl]),
                                    waits=vw, k=0, nd=True)
                                zam[kr] = vec(lambda e, esl=esl, slB=slB, sb=sb:
                                              e.tensor_copy(s_zaccf[:, sb, :],
                                                            s_feS[:, slB, esl]),
                                              nd=True)
                            else:
                                vw.append((sV, zam[kr - 2]))
                                vec(lambda e, esl=esl, slA=slA, sa=sa:
                                    e.tensor_add(s_zaccf[:, sa, :],
                                                 s_zaccf[:, sa, :],
                                                 s_feS[:, slA, esl]),
                                    waits=vw, k=0, nd=True)
                                zam[kr] = vec(lambda e, esl=esl, slB=slB, sb=sb:
                                              e.tensor_add(s_zaccf[:, sb, :],
                                                           s_zaccf[:, sb, :],
                                                           s_feS[:, slB, esl]),
                                              nd=True)
                        vec(lambda e: e.tensor_add(s_zaccf[:, 0, :],
                                                   s_zaccf[:, 0, :],
                                                   s_zaccf[:, 2, :]),
                            waits=[(sV, zam[4])], k=0, nd=True)
                        cmb = vec(lambda e: e.tensor_add(s_zaccf[:, 1, :],
                                                         s_zaccf[:, 1, :],
                                                         s_zaccf[:, 3, :]),
                                  waits=[(sV, zam[4])], nd=True)
                        cmb_m[u] = cmb
                        mm0(lambda e, u=u: e.matmul(
                            p_c[0:1, 128 * u:128 * u + 128], s_ones_colf[:],
                            s_zaccf[:, 0, :], start=(u == 0), stop=(u == 3),
                            tile_position=(0, 0)), waits=[(sV, cmb)])
                        P_op("tensor", lambda e, u=u: e.matmul(
                            p_d[32:33, 128 * u:128 * u + 128], s_ones_colf[:],
                            s_zaccf[:, 1, :], start=(u == 0), stop=(u == 3),
                            tile_position=(0, 32)), incs=[(sT, 1)])
                        zmmu[0] = sT.n
                        zpv_m[u] = sT.n

                    for u in range(4):
                        for kr in range(5):
                            kc = u + kr
                            kb = 512 * w + 128 * kc
                            qb0 = 512 * w + 128 * u
                            first_mm = (u == 0 and kr == 0)
                            psl = slice(128 * kr, 128 * kr + 128)
                            gstart = (kr == 0) or (kr == 4)
                            mlist = []
                            if kr == 0:
                                mlist.append(1)
                            if kr == 4:
                                mlist.append(0)
                            if w == 0 and kc < 4:
                                mlist.append(2)
                            gstop = (kr == 3) or (kr == 4)
                            qk_stop = gstop and not mlist
                            w0 = []
                            if first_mm:
                                w0 = [(sV, fcb_v), cur(sS),
                                      cur(sV), cur(sT)] + list(ghal)
                            elif kr == 0 and u >= 1:
                                w0 = [(sS, exp_m[u - 1])]
                            if kc < 4:
                                kap_a = s_fk2[0:64, kb - 512 * w:kb - 512 * w + 128]
                                kap_b = s_fk2[64:128, kb - 512 * w:kb - 512 * w + 128]
                            else:
                                kap_a = s_fk[0:64, pr, kb:kb + 128]
                                kap_b = s_fk[64:128, pr, kb:kb + 128]
                            if first_mm or (kr == 0 and u == 0):
                                w0 = list(w0) + [(sV, k2v)]
                            mm0(lambda e, qb0=qb0, psl=psl, pr=pr,
                                gstart=gstart, qk_stop=qk_stop, kap_a=kap_a:
                                e.matmul(p_qkA[:, psl], kap_a,
                                         s_fq[0:64, pr, qb0:qb0 + 128],
                                         start=gstart, stop=qk_stop,
                                         tile_position=(0, 0)), waits=w0)
                            km = [(sT, 1)] if not mlist else []
                            P_op("tensor", lambda e, qb0=qb0, psl=psl, pr=pr,
                                 gstart=gstart, qk_stop=qk_stop, kap_b=kap_b:
                                 e.matmul(p_qkB[:, psl], kap_b,
                                          s_fq[64:128, pr, qb0:qb0 + 128],
                                          start=gstart, stop=qk_stop,
                                          tile_position=(64, 0)), incs=km)
                            for mi, m in enumerate(mlist):
                                is_last = (mi == len(mlist) - 1)
                                mstop = gstop and is_last
                                mm0(lambda e, m=m, psl=psl, mstop=mstop: e.matmul(
                                    p_qkA[:, psl], s_ident[:], s_masks[:, m, :],
                                    start=False, stop=mstop,
                                    tile_position=(0, 0)))
                                mk = [(sT, 1)] if is_last else []
                                P_op("tensor", lambda e, m=m, psl=psl,
                                     mstop=mstop: e.matmul(
                                         p_qkB[:, psl], s_ident[:],
                                         s_masks[:, m, :], start=False,
                                         stop=mstop, tile_position=(0, 0)),
                                     incs=mk)
                        slA = 2 * (u % 2)
                        slB = 2 * (u % 2) + 1
                        we = [(sT, sT.n)]
                        if u >= 2:
                            we.append((sT, zpv_m[u - 2]))
                            we.append((sV, cmb_m[u - 2]))
                        sca(lambda e, slA=slA: e.activation(
                            s_feS[:, slA, :], p_qkA[:, 0:640],
                            AF.Exp, bias=s_fcb[:], scale=1.0), waits=we,
                            nd=True)
                        em = sca(lambda e, slB=slB: e.activation(
                            s_feS[:, slB, :], p_qkB[:, 0:640],
                            AF.Exp, bias=s_fcb[:], scale=1.0), nd=True)
                        exp_m[u] = em
                        if u >= 1:
                            emit_zpv_f(u - 1)
                    emit_zpv_f(3)
                    tz = sT.n
                    wsl = ts(w, 512)
                    vec(lambda e, w=w: e.tensor_copy(
                        s_zwf[0:1, 512 * w:512 * w + 512], p_c[0:1, :]),
                        waits=[(sT, tz)], nd=True)
                    vec(lambda e, w=w: e.tensor_copy(
                        s_zwf[32:33, 512 * w:512 * w + 512], p_d[32:33, :]),
                        nd=True)
                    vec(lambda e, pr=pr, wsl=wsl: e.tensor_copy(
                        s_fU[0:64, pr, wsl], p_a[0:64, :]), nd=True)
                    vec(lambda e, pr=pr, wsl=wsl: e.tensor_copy(
                        s_fU[64:128, pr, wsl], p_b[64:128, :]), nd=True)
                gza = dmaG(s_zpk2[:, 0:16], s_zwf[0:1, :], waits=[cur(sV)])
                gzb = dmaG(s_zpk2[:, 16:32], s_zwf[32:33, :])
                vz = vec(lambda e: e.reciprocal(s_zpk2[:], s_zpk2[:]),
                         waits=[gza, gzb])
                gz2a = dmaG(s_zwf[0:1, :], s_zpk2[:, 0:16], waits=[(sV, vz)])
                gz2b = dmaG(s_zwf[32:33, :], s_zpk2[:, 16:32])
                for w in range(4):
                    wsl = ts(w, 512)
                    tba = mm(lambda e, w=w: e.matmul(
                        p_c[0:64, :], s_ones_row_f[0:1, 0:64],
                        s_zwf[0:1, 512 * w:512 * w + 512], start=True,
                        stop=True, tile_position=(0, 0)),
                        waits=[gz2a, gz2b, cur(sV)])
                    tbb = mm(lambda e, w=w: e.matmul(
                        p_d[64:128, :], s_ones33[32:33, 0:64],
                        s_zwf[32:33, 512 * w:512 * w + 512], start=True,
                        stop=True, tile_position=(32, 64)))
                    vec(lambda e, pr=pr, wsl=wsl: e.tensor_mul(
                        s_fU[0:64, pr, wsl], s_fU[0:64, pr, wsl], p_c[0:64, :]),
                        waits=[(sT, tbb)] + ([cur(sV)] if w == 0 else []),
                        nd=True)
                    vec(lambda e, pr=pr, wsl=wsl: e.tensor_mul(
                        s_fU[64:128, pr, wsl], s_fU[64:128, pr, wsl],
                        p_d[64:128, :]), nd=True)
            wbanks = [p_qkA[:, 0:512], p_qkA[:, 512:1024],
                      p_qkB[:, 0:512], p_qkB[:, 512:1024],
                      p_a[:], p_b[:], p_c[:], p_d[:]]
            wo_marks = {}
            for tt in range(4):
                sl = ts(tt, 512)
                for ic in range(4):
                    wslot = wo_marks.get((tt, ic - 2)) or wo_marks.get(
                        (tt - 1, ic + 2))
                    wload = [(sT, wslot)] if wslot else [cur(sT)]
                    gwo = dmaG(s_fwo[:, ic % 2, :],
                               fw_o[l].rearrange("(c p) d -> p c d",
                                                 p=128)[:, ic, :],
                               waits=wload)
                    for d in range(NCH):
                        w0 = [gwo, cur(sV)] if d == 0 else []
                        km = [(sT, 1)] if d == NCH - 1 else []
                        P_op("tensor", lambda e, d=d, ic=ic, sl=sl:
                             e.matmul(wbanks[d], s_fwo[:, ic % 2, ts(d, 128)],
                                      s_fU[:, ic, sl], start=(ic == 0),
                                      stop=(ic == 3), tile_position=(0, 0)),
                             waits=w0, incs=km)
                    wo_marks[(tt, ic)] = sT.n
                two = sT.n
                for d in range(NCH):
                    vec(lambda e, d=d, tt=tt: e.scalar_tensor_tensor(
                        s_xf[:, d, 1 + 512 * tt:1 + 512 * (tt + 1)],
                        wbanks[d], 0.0,
                        s_xf[:, d, 1 + 512 * tt:1 + 512 * (tt + 1)],
                        ALU.add, ALU.add),
                        waits=[(sT, two)] if d == 0 else [], nd=True)
            flush()
            sc3.close()
            barrier()
            Ls.close()
            barrier()
            exch_col()
            flush()

            sc4 = ExitStack()
            s_xt2 = sc4.enter_context(
                nc.sbuf_tensor(f"s_xt2{l}", [128, NCH, 513], BF16))
            s_h = sc4.enter_context(nc.sbuf_tensor(f"s_h{l}", [128, NAB, 512], BF16))
            s_w1 = sc4.enter_context(
                nc.sbuf_tensor(f"s_w1{l}", [128, 2, 2, NCH, 128], BF16))
            s_w2 = sc4.enter_context(
                nc.sbuf_tensor(f"s_w2{l}", [128, 2, NCH, 128], BF16))
            s_gg = sc4.enter_context(nc.sbuf_tensor(f"s_gg{l}", [128, 512], BF16))
            s_xnm = sc4.enter_context(
                nc.sbuf_tensor(f"s_xnm{l}", [128, NCH, T], BF16))

            def xt2_sl(c, t0, n):
                off = 0 if c >= 4 else 1
                return s_xt2[:, c, off + t0: off + t0 + n]

            for tt in range(4):
                gxt = dmaG(s_xt2[:], s_xf[:, :, 512 * tt:512 * tt + 513],
                           waits=[cur(sV), cur(sT)])
                stats_tile(lambda c: xt2_sl(c, 0, 512),
                           s_rowm[:, ts(tt, 512)], s_rowq[:, ts(tt, 512)],
                           [gxt])
            g2 = ln_rows(16, double_ln=False)
            for tt in range(4):
                sl = ts(tt, 512)
                mm(lambda e, tt=tt: e.matmul(p_c[:], s_ones_row_f[:],
                                             s_rowm[:, ts(tt, 512)], start=True,
                                             stop=True, tile_position=(0, 0)),
                   waits=list(g2) + [cur(sV)])
                tB = mm(lambda e, tt=tt: e.matmul(p_d[:], s_ones_row_f[:],
                                                  s_rowq[:, ts(tt, 512)],
                                                  start=True, stop=True,
                                                  tile_position=(0, 0)))
                for c in range(NCH):
                    off = 0 if c >= 4 else 1
                    w0 = [(sT, tB)] if c == 0 else []
                    vec(lambda e, c=c, tt=tt, off=off: e.tensor_mul(
                        s_x32[:, c, :],
                        s_xf[:, c, off + 512 * tt:off + 512 * tt + 512], p_c[:]),
                        waits=w0, k=0)
                    vec(lambda e, c=c: e.tensor_add(s_x32[:, c, :],
                                                    s_x32[:, c, :], p_d[:]), k=0)
                    vec(lambda e, c=c, sl=sl: e.tensor_scalar(
                        s_xnm[:, c, sl], s_x32[:, c, :],
                        s_lgb2[:, 0, c:c + 1], s_lgb2[:, 1, c:c + 1],
                        ALU.mult, ALU.add), k=(1 if c == NCH - 1 else 0))
            w1_mm_marks = {}
            w2_mm_marks = {}
            for tt in range(4):
                sl = ts(tt, 512)
                vln = sV.n
                gel_marks = {}
                for k in range(NAB):
                    wdt = 128 if k < NAB - 1 else LAST
                    wslot = w1_mm_marks.get((tt, k - 2)) or w1_mm_marks.get(
                        (tt - 1, k + 20))
                    wload = [(sT, wslot)] if wslot else [cur(sT)]
                    ga_ = [dmaG(s_w1[:, k % 2, 0, :, 0:wdt],
                                fw_1[l].rearrange("(c p) i -> p c i", p=128)
                                [:, :, 128 * k:128 * k + wdt], waits=wload),
                           dmaG(s_w1[:, k % 2, 1, :, 0:wdt],
                                fw_1[l].rearrange("(c p) i -> p c i", p=128)
                                [:, :, HID + 128 * k:HID + 128 * k + wdt])]
                    hfA = ts(k % 2, 512)
                    for c in range(NCH):
                        w0 = ga_ + [(sV, vln)] if c == 0 else []
                        if k >= 2 and c == 0:
                            w0 = list(w0) + [(sS, gel_marks[k - 2][0]),
                                             (sV, gel_marks[k - 2][1])]
                        mm0(lambda e, k=k, c=c, wdt=wdt, hfA=hfA, sl=sl:
                            e.matmul(p_qkA[0:wdt, hfA], s_w1[:, k % 2, 0, c, 0:wdt],
                                     s_xnm[:, c, sl], start=(c == 0),
                                     stop=(c == NCH - 1), tile_position=(0, 0)),
                            waits=w0)
                        km = [(sT, 1)] if c == NCH - 1 else []
                        P_op("tensor", lambda e, k=k, c=c, wdt=wdt, hfA=hfA, sl=sl:
                             e.matmul(p_qkB[0:wdt, hfA], s_w1[:, k % 2, 1, c, 0:wdt],
                                      s_xnm[:, c, sl], start=(c == 0),
                                      stop=(c == NCH - 1), tile_position=(0, 0)),
                             incs=km)
                    tk_ = sT.n
                    w1_mm_marks[(tt, k)] = tk_
                    wg = [(sT, tk_)]
                    if k >= 1:
                        wg.append((sV, gel_marks[k - 1][1]))
                    sg = sca(lambda e, k=k, wdt=wdt, hfA=hfA: e.activation(
                        s_gg[0:wdt, :], p_qkB[0:wdt, hfA], AF.Gelu,
                        bias=s_b1[0:wdt, 1, k:k + 1], scale=1.0), waits=wg,
                        nd=True)
                    vg = vec(lambda e, k=k, wdt=wdt, hfA=hfA: e.scalar_tensor_tensor(
                        s_h[0:wdt, k, :], p_qkA[0:wdt, hfA],
                        s_b1[0:wdt, 0, k:k + 1],
                        s_gg[0:wdt, :], ALU.add, ALU.mult),
                        waits=[(sS, sg)], nd=True)
                    gel_marks[k] = (sg, vg)
                vh = sV.n
                banks = [p_qkA[:, 0:512], p_qkA[:, 512:1024],
                         p_qkB[:, 0:512], p_qkB[:, 512:1024],
                         p_a[:], p_b[:], p_c[:], p_d[:]]
                for k in range(NAB):
                    wdt = 128 if k < NAB - 1 else LAST
                    wslot = w2_mm_marks.get((tt, k - 2)) or w2_mm_marks.get(
                        (tt - 1, k + 20))
                    wload = [(sT, wslot)] if wslot else [cur(sT)]
                    gw2 = dmaG(s_w2[0:wdt, k % 2, :, :],
                               fw_2[l][128 * k:128 * k + wdt, :].rearrange(
                                   "p (c d) -> p c d", d=128), waits=wload)
                    for d in range(NCH):
                        w0 = [gw2, (sV, vh)] if d == 0 else []
                        km = [(sT, 1)] if (d == NCH - 1) else []
                        P_op("tensor", lambda e, k=k, d=d, wdt=wdt: e.matmul(
                            banks[d], s_w2[0:wdt, k % 2, d, :], s_h[0:wdt, k, :],
                            start=(k == 0), stop=(k == NAB - 1),
                            tile_position=(0, 0)), waits=w0, incs=km)
                tw2 = sT.n
                for d in range(NCH):
                    vec(lambda e, d=d, tt=tt: e.scalar_tensor_tensor(
                        s_xf[:, d, 1 + 512 * tt:1 + 512 * (tt + 1)],
                        banks[d], s_b2f[:, d:d + 1],
                        s_xf[:, d, 1 + 512 * tt:1 + 512 * (tt + 1)],
                        ALU.add, ALU.add), waits=[(sT, tw2)] if d == 0 else [],
                        k=(1 if d == NCH - 1 else 0), nd=True)
            flush()
            sc4.close()
            barrier()
            if l + 1 < n_ffn:
                exch_col()
                flush()

        for tt in range(4):
            sl = ts(tt, 512)
            hA = dmaH(s_x32[:], spill3[:, :, sl], waits=[cur(sV), *HALL()])
            for c in range(NCH):
                w0 = [hA] if c == 0 else []
                vec(lambda e, c=c, tt=tt: e.tensor_add(
                    s_x32[:, c, :], s_x32[:, c, :],
                    s_xf[:, c, 1 + 512 * tt:1 + 512 * (tt + 1)]),
                    waits=w0, k=(1 if c == NCH - 1 else 0), nd=True)
            vv = sV.n
            dmaH(out3[:, :, sl], s_x32[:], waits=[(sV, vv)])
        flush()
        fd.close()
    else:
        for tt in range(4):
            sl = ts(tt, 512)
            hA = dmaH(s_x32[:], spill3[:, :, sl], waits=[cur(sV), *HALL()])
            dmaH(out3[:, :, sl], s_x32[:], waits=[hA])
        flush()

    P_op("sync", lambda e: e.nop(), waits=HALL() + GALL())
    P_op("gpsimd", lambda e: e.nop(), waits=HALL() + GALL())
    flush()

    es.close()
    return nc



def kernel(**inputs):
    x = np.asarray(inputs["L0_attn_x"], np.float32)
    n_cores = 8
    nc = build()
    def pck(v):
        return np.ascontiguousarray(np.asarray(v, np.float32).reshape(NCH, 128).T)

    shared = {
        "ln_gb": np.ascontiguousarray(np.stack(
            [pck(inputs["L0_attn_ln_gamma"]), pck(inputs["L0_attn_ln_beta"])], axis=1)),
        "Wq": np.ascontiguousarray(np.asarray(inputs["L0_attn_Wq"], np.float32)),
        "Wk": np.ascontiguousarray(np.asarray(inputs["L0_attn_Wk"], np.float32)),
        "Wv": np.ascontiguousarray(np.asarray(inputs["L0_attn_Wv"], np.float32)),
        "Wo": np.ascontiguousarray(np.asarray(inputs["L0_attn_Wo"], np.float32)),
    }
    for l in range(2):
        shared[f"fw_qkv{l}"] = np.ascontiguousarray(np.asarray(inputs["L0_ffn_wqkv"], np.float32)[l])
        qsl_ = np.asarray(inputs["L0_ffn_q_scale"], np.float32)[l]
        ksl_ = np.asarray(inputs["L0_ffn_k_scale"], np.float32)[l]
        shared[f"f_qsv{l}"] = np.ascontiguousarray(np.stack(
            [np.concatenate([qsl_, qsl_]), np.concatenate([ksl_, ksl_])], axis=1))
        shared[f"fw_o{l}"] = np.ascontiguousarray(np.asarray(inputs["L0_ffn_wo"], np.float32)[l])
        shared[f"f_lgb{l}"] = np.ascontiguousarray(np.stack(
            [pck(np.asarray(inputs["L0_ffn_ln_g"], np.float32)[l]),
             pck(np.asarray(inputs["L0_ffn_ln_b"], np.float32)[l])], axis=1))
        shared[f"fw_1{l}"] = np.ascontiguousarray(np.asarray(inputs["L0_ffn_w1"], np.float32)[l])
        b1l = np.asarray(inputs["L0_ffn_b1"], np.float32)[l]
        b1arr = np.zeros((128, 2, NAB), np.float32)
        for half_ in range(2):
            v_ = b1l[half_ * HID:(half_ + 1) * HID]
            b1arr[:, half_, 0:21] = v_[0:2688].reshape(21, 128).T
            b1arr[0:LAST, half_, 21] = v_[2688:2730]
        shared[f"f_b1v{l}"] = np.ascontiguousarray(b1arr)
        shared[f"fw_2{l}"] = np.ascontiguousarray(np.asarray(inputs["L0_ffn_w2"], np.float32)[l])
        shared[f"f_b2v{l}"] = np.ascontiguousarray(pck(np.asarray(inputs["L0_ffn_b2"], np.float32)[l]))
    in_maps = []
    for c in range(n_cores):
        b, hh = c // 2, c % 2
        xb = x[b]
        own = xb[T * hh:T * hh + T].T
        oth = xb[T * (1 - hh):T * (1 - hh) + T].T
        d = dict(shared)
        d["xfm"] = np.ascontiguousarray(np.concatenate([own, oth], axis=1))
        d.update(host_constants(c))
        in_maps.append(d)
    res = run_bass_kernel_spmd(nc, in_maps, core_ids=list(range(n_cores)))
    out = np.empty((B, S, DM), np.float32)
    for c in range(n_cores):
        b, hh = c // 2, c % 2
        out[b, T * hh:T * hh + T] = res.results[c]["out_fm"].T
    return out


# version 10
# speedup vs baseline: 1.1664x; 1.1664x over previous
import sys
import numpy as np
from contextlib import ExitStack

sys.path.insert(0, "/opt/trn_rl_repo")

import concourse.bass as bass
import concourse.mybir as mybir
from concourse.bass_utils import run_bass_kernel_spmd

F32 = mybir.dt.float32
BF16 = mybir.dt.bfloat16
AF = mybir.ActivationFunctionType
ALU = mybir.AluOpType
AX = mybir.AxisListType
ts = bass.ts

B, S, T, DM, NCH = 4, 4096, 2048, 1024, 8
H, DH = 16, 64
NEG = -1e30
EPS = 1e-5
FINNER, WIN, HID, NAB = 512, 512, 2730, 22
LAST = HID - 128 * (NAB - 1)
QK_SCALE = 8.0
SLOPES = [2.0 ** (-8.0 / (16 - h)) for h in range(H)]

N_FFN_LAYERS = 2
DEBUG_TAPS = 0


def host_constants(core):
    hh = core % 2
    q_base = T * hh
    other_base = T * (1 - hh)
    C = {}

    ident = np.eye(128, dtype=np.float32)
    C["c_ident"] = ident
    C["c_neg2id"] = -2.0 * ident
    C["c_ones_col"] = np.ones((128, 1), np.float32)
    C["c_ones_colf"] = np.ones((128, 1), np.float32)
    C["c_ones_row_f"] = np.ones((1, 128), np.float32)
    C["c_ones33"] = np.ones((33, 128), np.float32)
    blk2 = np.zeros((128, 2), np.float32)
    blk2[0:64, 0] = 1.0
    blk2[64:128, 1] = 1.0
    C["c_blk2"] = blk2
    ind2 = np.zeros((2, 128), np.float32)
    ind2[0, 0:64] = 1.0
    ind2[1, 64:128] = 1.0
    C["c_ind2"] = ind2

    al = np.zeros((35, 4 * 32, 128), np.float32)
    a_idx = np.arange(128, dtype=np.float32)
    b_idx = np.arange(512, dtype=np.float32)
    for qt in range(4):
        q0 = q_base + 512 * qt
        for kc in range(32):
            j0 = q_base + 128 * kc if kc < 16 else other_base + 128 * (kc - 16)
            v = qt * 32 + kc
            s = -1.0 if j0 >= q0 + 512 else 1.0
            Cc = q0 - j0 + 192
            al[0, v, :] = s
            al[1, v, :] = s * (a_idx - 64.0)
            al[2, v, :] = -s * Cc
            al[32:35, v, :] = al[0:3, v, :]
    C["c_alibi_lhsT"] = al.reshape(35, 4 * 32 * 128)

    rhs = np.zeros((35, 512), np.float32)
    rhs[0] = -(b_idx - 256.0)
    rhs[1] = 1.0
    rhs[2] = 1.0
    rhs[32:35] = rhs[0:3]
    C["c_alibi_rhs"] = rhs

    Rt = np.zeros((128, 4, 512), np.float32)
    for di, d in enumerate((0, 128, 256, 384)):
        Rt[:, di, :] = np.maximum(d + a_idx[:, None] - b_idx[None, :], 0.0)
    C["c_R"] = Rt.reshape(128, 4 * 512)

    qs = np.zeros((128, 8), np.float32)
    for p in range(8):
        qs[0:64, p] = 0.125 / SLOPES[2 * p]
        qs[64:128, p] = 0.125 / SLOPES[2 * p + 1]
    C["c_qscale"] = qs
    sq = np.zeros((128, 8), np.float32)
    for p in range(8):
        sq[0, p] = np.sqrt(SLOPES[2 * p])
        sq[1, p] = np.sqrt(SLOPES[2 * p + 1])
    C["c_sqslope"] = sq

    t_glob = np.arange(q_base, q_base + T, dtype=np.float64)
    half = np.arange(0, DH, 2, dtype=np.float64)
    inv_freq = 1.0 / 10000.0 ** (half / DH)
    freqs = np.concatenate([t_glob[:, None] * inv_freq[None, :]] * 2, axis=1)
    base = (half + 0.4 * DH) / (1.4 * DH)
    power = ((t_glob % WIN)) / float(WIN // 2)
    scale = np.concatenate([base[None, :] ** power[:, None]] * 2, axis=1)
    cos, sin = np.cos(freqs), np.sin(freqs)
    sgn = np.ones((DH,))
    sgn[0:32] = -1.0

    def dup(t):
        t64 = t.T
        return np.ascontiguousarray(
            np.concatenate([t64, t64], axis=0).astype(np.float32))

    C["c_rot_qc"] = dup(cos * scale) * QK_SCALE
    C["c_rot_qs"] = dup(sin * scale * sgn[None, :]) * QK_SCALE
    C["c_rot_kc"] = dup(cos / scale)
    C["c_rot_ks"] = dup(sin / scale * sgn[None, :])
    b2v = np.concatenate([base ** 2, base ** 2])
    C["c_base2"] = np.ascontiguousarray(
        np.concatenate([b2v, b2v]).astype(np.float32).reshape(128, 1))

    aa = a_idx[:, None]
    bb = np.arange(128, dtype=np.float32)[None, :]
    C["c_mask_d0"] = np.where(aa > bb, NEG, 0.0).astype(np.float32)
    C["c_mask_dm512"] = np.where(aa < bb, NEG, 0.0).astype(np.float32)
    C["c_mask_w0"] = np.full((128, 128), NEG if hh == 0 else 0.0, np.float32)
    C["c_gate"] = np.full((128, 1), float(hh), np.float32)
    return C


class Sem:
    def __init__(self, handle, name):
        self.h, self.name, self.n = handle, name, 0


class Plan:
    ENGINES = ("sync", "gpsimd", "tensor", "scalar", "vector")

    def __init__(self):
        self.items = []

    def op(self, eng, fn, waits=(), incs=(), nd=False):
        ws = []
        for w in waits:
            if isinstance(w, Sem):
                ws.append((w, w.n))
            else:
                ws.append((w[0], w[1]))
        incs = [i for i in incs if i[1] > 0]
        self.items.append((eng, ws, fn, incs, nd))
        for s, k in incs:
            s.n += k

    def emit_into(self, block):
        for name in self.ENGINES:
            if not any(e == name for e, _, _, _, _ in self.items):
                continue
            getattr(block, name)(lambda eng, name=name: self._emit(name, eng))

    def _emit(self, name, eng):
        autodrain = name in ("vector", "scalar")
        for e, ws, fn, incs, nd in self.items:
            if e != name:
                continue
            for s, v in ws:
                if v > 0:
                    eng.wait_ge(s.h, v)
            ins = fn(eng)
            for s, k in incs:
                ins = ins.then_inc(s.h, k)
            if autodrain and not nd:
                eng.drain()


def build(n_ffn=N_FFN_LAYERS, taps=DEBUG_TAPS):
    nc = bass.Bass("TRN2", target_bir_lowering=False, debug=False)
    DT = nc.dram_tensor

    def IN(name, shape):
        return DT(name, shape, F32, kind="ExternalInput").ap()

    xfm = IN("xfm", [DM, S])
    ln_gb = IN("ln_gb", [128, 2, NCH])
    Wq, Wk, Wv, Wo = (IN(n, [DM, DM]) for n in ("Wq", "Wk", "Wv", "Wo"))
    fw_qkv = [IN(f"fw_qkv{l}", [DM, 3 * FINNER]) for l in range(2)]
    f_qsv = [IN(f"f_qsv{l}", [128, 2]) for l in range(2)]
    fw_o = [IN(f"fw_o{l}", [FINNER, DM]) for l in range(2)]
    f_lgb = [IN(f"f_lgb{l}", [128, 2, NCH]) for l in range(2)]
    fw_1 = [IN(f"fw_1{l}", [DM, 2 * HID]) for l in range(2)]
    f_b1v = [IN(f"f_b1v{l}", [128, 2, NAB]) for l in range(2)]
    fw_2 = [IN(f"fw_2{l}", [HID, DM]) for l in range(2)]
    f_b2v = [IN(f"f_b2v{l}", [128, NCH]) for l in range(2)]

    c_ident = IN("c_ident", [128, 128])
    c_neg2id = IN("c_neg2id", [128, 128])
    c_ones_col = IN("c_ones_col", [128, 1])
    c_ones_colf = IN("c_ones_colf", [128, 1])
    c_ones_row_f = IN("c_ones_row_f", [1, 128])
    c_ones33 = IN("c_ones33", [33, 128])
    c_blk2 = IN("c_blk2", [128, 2])
    c_ind2 = IN("c_ind2", [2, 128])
    c_alibi_lhsT = IN("c_alibi_lhsT", [35, 4 * 32 * 128])
    c_alibi_rhs = IN("c_alibi_rhs", [35, 512])
    c_R = IN("c_R", [128, 4 * 512])
    c_qscale = IN("c_qscale", [128, 8])
    c_sqslope = IN("c_sqslope", [128, 8])
    c_rot = {k: IN("c_rot_" + k, [128, T]) for k in ("qc", "qs", "kc", "ks")}
    c_mask_d0 = IN("c_mask_d0", [128, 128])
    c_mask_dm512 = IN("c_mask_dm512", [128, 128])
    c_mask_w0 = IN("c_mask_w0", [128, 128])
    c_gate = IN("c_gate", [128, 1])
    c_base2 = IN("c_base2", [128, 1])

    out_fm = DT("out_fm", [DM, T], F32, kind="ExternalOutput").ap()
    tap_resid = DT("tap_resid", [DM, T], F32, kind="ExternalOutput").ap() if taps else None
    tap_xn = DT("tap_xn", [DM, S], F32, kind="ExternalOutput").ap() if taps else None
    d_spill = DT("d_spill", [DM, T], F32).ap()
    d_snd_col = DT("d_snd_col", [128, NCH], F32)
    d_rcv_col = DT("d_rcv_col", [256, NCH], F32)
    d_snd_kv = DT("d_snd_kv", [1024, 512], F32)
    d_rcv_kv = DT("d_rcv_kv", [2048, 512], F32)

    es = ExitStack()
    E = es.enter_context
    SB = lambda name, shape, dt=BF16: E(nc.sbuf_tensor(name, shape, dt))
    PSUM = lambda name, shape: E(nc.psum_tensor(name, shape, F32))

    s_ident = SB("s_ident", [128, 128])
    s_identf = SB("s_identf", [128, 128], F32)
    s_neg2id = SB("s_neg2id", [128, 128])
    s_ones_col = SB("s_ones_col", [128, 1])
    s_ones_colf = SB("s_ones_colf", [128, 1], F32)
    s_ones_row_f = SB("s_ones_row_f", [1, 128], F32)
    s_ones33 = SB("s_ones33", [33, 128], F32)
    s_blk2 = SB("s_blk2", [128, 2])
    s_ind2 = SB("s_ind2", [2, 128])
    s_al_rhs = SB("s_al_rhs", [35, 512])
    s_qscale = SB("s_qscale", [128, 8], F32)
    s_sqslope = SB("s_sqslope", [128, 8], F32)
    s_gate = SB("s_gate", [128, 1], F32)
    s_base2 = SB("s_base2", [128, 1], F32)
    s_lngb = SB("s_lngb", [128, 2, NCH], F32)
    s_masks = SB("s_masks", [128, 3, 128])
    s_x32 = SB("s_x32", [128, NCH, 512], F32)
    s_xsq = SB("s_xsq", [128, NCH, 512])
    s_rowm = SB("s_rowm", [1, 2048], F32)
    s_rowq = SB("s_rowq", [1, 2048], F32)
    s_pk = SB("s_pk", [128, 64], F32)
    s_tiny = SB("s_tiny", [128, 8], F32)
    s_trow = SB("s_trow", [1, 128], F32)
    s_cbias = SB("s_cbias", [128, 1], F32)
    s_xcol = SB("s_xcol", [128, NCH, 1], F32)

    p_qkA = PSUM("p_qkA", [128, 1024])
    p_qkB = PSUM("p_qkB", [128, 1024])
    p_a = PSUM("p_a", [128, 512])
    p_b = PSUM("p_b", [128, 512])
    p_c = PSUM("p_c", [128, 512])
    p_d = PSUM("p_d", [128, 512])

    sems = {}
    for nm in ("T", "Sc", "V", "CC"):
        sems[nm] = Sem(E(nc.semaphore("sem_" + nm)), nm)
    sT, sS, sV, sCC = (sems[k] for k in ("T", "Sc", "V", "CC"))
    NHS, NGS = 4, 6
    dH_slots = [Sem(E(nc.semaphore(f"sem_dH{i}")), f"dH{i}") for i in range(NHS)]
    dG_slots = [Sem(E(nc.semaphore(f"sem_dG{i}")), f"dG{i}") for i in range(NGS)]
    slot_idx = {"H": 0, "G": 0}
    ALLSEMS = [sT, sS, sV] + dH_slots + dG_slots

    def HALL():
        return [(s, s.n) for s in dH_slots]

    def GALL():
        return [(s, s.n) for s in dG_slots]

    block = E(nc.Block())
    state = {"P": Plan()}

    def P_op(eng, fn, waits=(), incs=(), nd=False):
        state["P"].op(eng, fn, waits, incs, nd)

    def flush():
        state["P"].emit_into(block)
        state["P"] = Plan()

    def barrier():
        for eng in Plan.ENGINES:
            P_op(eng, lambda e: e.nop(), waits=[(s, s.n) for s in ALLSEMS])
        flush()

    cur = lambda s: (s, s.n)

    def dmaH(dst, src, waits=()):
        sl_ = dH_slots[slot_idx["H"] % NHS]
        slot_idx["H"] += 1
        P_op("sync", lambda e, d=dst, s=src: e.dma_start(out=d, in_=s),
             waits=list(waits) + [(sl_, sl_.n)], incs=[(sl_, 16)])
        return (sl_, sl_.n)

    def dmaG(dst, src, waits=(), slow=False):
        sg_ = dG_slots[slot_idx["G"] % NGS]
        slot_idx["G"] += 1
        P_op("gpsimd", lambda e, d=dst, s=src, sl2=slow: e.dma_start(
            out=d, in_=s, allow_slow_non_contiguous=sl2),
             waits=list(waits) + [(sg_, sg_.n)], incs=[(sg_, 16)])
        return (sg_, sg_.n)

    def vec(fn, waits=(), k=1, nd=False):
        P_op("vector", fn, waits=waits, incs=[(sV, k)], nd=nd)
        return sV.n

    def sca(fn, waits=(), k=1, nd=False):
        P_op("scalar", fn, waits=waits, incs=[(sS, k)], nd=nd)
        return sS.n

    def mm(fn, waits=(), k=1):
        P_op("tensor", fn, waits=waits, incs=[(sT, k)])
        return sT.n

    def mm0(fn, waits=()):
        P_op("tensor", fn, waits=waits)

    xfm3 = xfm.rearrange("(c p) t -> p c t", p=128)
    spill3 = d_spill.rearrange("(c p) t -> p c t", p=128)
    out3 = out_fm.rearrange("(c p) t -> p c t", p=128)
    
    dmaG(s_ident[:], c_ident)
    dmaH(s_identf[:], c_ident)
    dmaG(s_neg2id[:], c_neg2id)
    dmaG(s_ones_col[:], c_ones_col)
    dmaH(s_ones_colf[:], c_ones_colf)
    dmaH(s_ones_row_f[:], c_ones_row_f)
    dmaH(s_ones33[:], c_ones33)
    dmaG(s_blk2[:], c_blk2)
    dmaG(s_ind2[:], c_ind2)
    dmaG(s_al_rhs[:], c_alibi_rhs)
    dmaH(s_qscale[:], c_qscale)
    dmaH(s_sqslope[:], c_sqslope)
    dmaH(s_gate[:], c_gate)
    dmaH(s_base2[:], c_base2)
    dmaH(s_lngb[:], ln_gb)
    dmaG(s_masks[:, 0, :], c_mask_d0)
    dmaG(s_masks[:, 1, :], c_mask_dm512)
    dmaG(s_masks[:, 2, :], c_mask_w0)
    barrier()

    def stats_tile(src_fn, dst_m, dst_q, wait_first):
        for c in range(NCH):
            vec(lambda e, c=c: e.tensor_mul(s_xsq[:, c, :], src_fn(c), src_fn(c)),
                waits=(list(wait_first) + [cur(sT), cur(sV)]) if c == 0 else [],
                k=(1 if c == NCH - 1 else 0), nd=True)
        v1 = sV.n
        for c in range(NCH):
            w0 = (list(wait_first) + [cur(sV)]) if c == 0 else []
            mm0(lambda e, c=c: e.matmul(
                p_a[0:1, :], s_ones_col[:], src_fn(c), start=(c == 0),
                stop=(c == NCH - 1), tile_position=(0, 0)), waits=w0)
        for c in range(NCH):
            w0 = [(sV, v1)] if c == 0 else []
            km = [(sT, 1)] if c == NCH - 1 else []
            P_op("tensor", lambda e, c=c: e.matmul(
                p_b[0:1, :], s_ones_col[:], s_xsq[:, c, :], start=(c == 0),
                stop=(c == NCH - 1), tile_position=(0, 0)), waits=w0, incs=km)
        tq = sT.n
        vec(lambda e: e.tensor_copy(dst_m, p_a[0:1, :]), waits=[(sT, tq)],
            nd=True)
        return vec(lambda e: e.tensor_copy(dst_q, p_b[0:1, :]), nd=True)

    def ln_rows(W, double_ln):
        g1a = dmaG(s_pk[:, 0:W], s_rowm[:, 0:128 * W], waits=[cur(sV)])
        g1b = dmaG(s_pk[:, W:2 * W], s_rowq[:, 0:128 * W])

        def cm(e, W=W):
            m = s_pk[:, 0:W]
            q2 = s_pk[:, W:2 * W]
            t1 = s_pk[:, 2 * W:3 * W]
            e.tensor_scalar_mul(m, m, 1.0 / DM)
            e.drain()
            e.tensor_scalar_mul(q2, q2, 1.0 / DM)
            e.drain()
            e.tensor_mul(t1, m, m)
            e.drain()
            e.tensor_sub(q2, q2, t1)
            e.drain()
            return e.tensor_scalar_add(t1, q2, EPS)
        vec(cm, waits=[g1a, g1b], nd=True)
        sca(lambda e, W=W: e.activation(s_pk[:, 3 * W:4 * W],
                                        s_pk[:, 2 * W:3 * W], AF.Sqrt),
            waits=[cur(sV)], nd=True)
        if double_ln:
            def cm2(e, W=W):
                r1 = s_pk[:, 2 * W:3 * W]
                e.reciprocal(r1, s_pk[:, 3 * W:4 * W])
                e.drain()
                t2 = s_pk[:, 3 * W:4 * W]
                e.tensor_mul(t2, r1, r1)
                e.drain()
                e.tensor_mul(t2, s_pk[:, W:2 * W], t2)
                e.drain()
                return e.tensor_scalar_add(t2, t2, EPS)
            vec(cm2, waits=[cur(sS)], nd=True)
            sca(lambda e, W=W: e.activation(s_pk[:, W:2 * W],
                                            s_pk[:, 3 * W:4 * W], AF.Sqrt),
                waits=[cur(sV)], nd=True)

            def cm3(e, W=W):
                r2 = s_pk[:, 3 * W:4 * W]
                e.reciprocal(r2, s_pk[:, W:2 * W])
                e.drain()
                rho = s_pk[:, W:2 * W]
                e.tensor_mul(rho, s_pk[:, 2 * W:3 * W], r2)
                e.drain()
                return e.scalar_tensor_tensor(s_pk[:, 2 * W:3 * W], s_pk[:, 0:W],
                                              -1.0, rho, ALU.mult, ALU.mult)
            vec(cm3, waits=[cur(sS)], nd=True)
        else:
            def cm3b(e, W=W):
                rho = s_pk[:, W:2 * W]
                e.reciprocal(rho, s_pk[:, 3 * W:4 * W])
                e.drain()
                return e.scalar_tensor_tensor(s_pk[:, 2 * W:3 * W], s_pk[:, 0:W],
                                              -1.0, rho, ALU.mult, ALU.mult)
            vec(cm3b, waits=[cur(sS)], nd=True)
        g2a = dmaG(s_rowm[:, 0:128 * W], s_pk[:, W:2 * W], waits=[cur(sV)])
        g2b = dmaG(s_rowq[:, 0:128 * W], s_pk[:, 2 * W:3 * W])
        return [g2a, g2b]

    def norm_accum(src_fn, n_t, idx, first):
        if first:
            vec(lambda e: e.memset(s_tiny[:], 0.0), waits=[cur(sS)])
        for t2 in range(n_t):
            vsq = vec(lambda e, t2=t2: e.tensor_mul(
                s_xsq[:, 0, :], src_fn(t2), src_fn(t2)),
                waits=[cur(sV), cur(sS)] if t2 == 0 else [], nd=True)
            tn = mm(lambda e: e.matmul(p_c[0:2, :], s_blk2[:], s_xsq[:, 0, :],
                                       start=True, stop=True,
                                       tile_position=(0, 0)), waits=[(sV, vsq)])
            vec(lambda e: e.reduce_max(s_tiny[0:2, 2:3], p_c[0:2, :], axis=AX.X),
                waits=[(sT, tn)])
            vec(lambda e, idx=idx: e.tensor_max(
                s_tiny[0:2, idx:idx + 1], s_tiny[0:2, idx:idx + 1],
                s_tiny[0:2, 2:3]))

    def bound_bias(dst, extra_scale):
        sca(lambda e: e.activation(s_tiny[:, 4:6], s_tiny[:, 0:2], AF.Sqrt),
            waits=[cur(sV)])
        sca(lambda e, xs=extra_scale: e.activation(
            s_tiny[:, 4:6], s_tiny[:, 4:6], AF.Copy, scale=xs))
        vec(lambda e: e.tensor_mul(s_tiny[0:2, 6:7], s_tiny[0:2, 4:5],
                                   s_tiny[0:2, 5:6]), waits=[cur(sS)])
        tt_ = mm(lambda e: e.transpose(p_c[0:1, 0:128], s_tiny[:, 6:7], s_identf[:]),
                 waits=[cur(sV)])
        vec(lambda e: e.reduce_max(s_trow[:, 0:1], p_c[0:1, 0:2], axis=AX.X),
            waits=[(sT, tt_)])
        vec(lambda e: e.tensor_scalar(s_trow[:, 1:2], s_trow[:, 0:1], -1.0, -1.0,
                                      ALU.mult, ALU.add))
        tc = mm(lambda e: e.matmul(p_c[:, 0:1], s_ones_row_f[:], s_trow[:, 1:2],
                                   start=True, stop=True, tile_position=(0, 0)),
                waits=[cur(sV)])
        return vec(lambda e, dst=dst: e.tensor_copy(dst, p_c[:, 0:1]),
                   waits=[(sT, tc)])

    ab = ExitStack()
    A = ab.enter_context
    s_xn = A(nc.sbuf_tensor("s_xn", [128, NCH, S], BF16))
    s_U = A(nc.sbuf_tensor("s_U", [128, NCH, T], BF16))
    s_K = A(nc.sbuf_tensor("s_K", [128, S], BF16))
    s_Q = A(nc.sbuf_tensor("s_Q", [128, T], BF16))
    s_Vt = A(nc.sbuf_tensor("s_Vt", [128, 32, 128], BF16))
    s_Vf = A(nc.sbuf_tensor("s_Vf", [128, 2, 512], F32))
    s_eS = A(nc.sbuf_tensor("s_eS", [128, 2, 2, 512], BF16))
    s_al_lhsT = A(nc.sbuf_tensor("s_al_lhsT", [35, 32, 128], BF16))
    s_R = A(nc.sbuf_tensor("s_R", [128, 4, 512], BF16))
    s_wq = A(nc.sbuf_tensor("s_wq", [128, NCH, 128], BF16))
    s_wk = A(nc.sbuf_tensor("s_wk", [128, NCH, 128], BF16))
    s_wv = A(nc.sbuf_tensor("s_wv", [128, NCH, 128], BF16))
    s_wo2 = A(nc.sbuf_tensor("s_wo2", [128, 2, NCH, 128], BF16))
    s_zw = A(nc.sbuf_tensor("s_zw", [64, 2048], F32))
    s_zacc = A(nc.sbuf_tensor("s_zacc", [128, 4, 512], F32))
    s_zpk = A(nc.sbuf_tensor("s_zpk", [128, 32], F32))

    dmaG(s_R[:], c_R.rearrange("p (d n) -> p d n", d=4))

    for half_i in range(2):
        base = 2048 * half_i
        for tt in range(4):
            sl = slice(base + 512 * tt, base + 512 * (tt + 1))
            gB = dmaG(s_xn[:, :, sl], xfm3[:, :, sl], waits=[cur(sT)])
            stats_tile(lambda c, sl=sl: s_xn[:, c, sl],
                       s_rowm[:, ts(tt, 512)], s_rowq[:, ts(tt, 512)],
                       [gB])
        g2 = ln_rows(16, double_ln=True)
        for tt in range(4):
            sl = slice(base + 512 * tt, base + 512 * (tt + 1))
            hA = dmaH(s_x32[:], xfm3[:, :, sl], waits=[cur(sV), *HALL()])
            mm(lambda e, tt=tt: e.matmul(p_c[:], s_ones_row_f[:],
                                         s_rowm[:, ts(tt, 512)], start=True,
                                         stop=True, tile_position=(0, 0)),
               waits=list(g2) + [cur(sV)])
            tB = mm(lambda e, tt=tt: e.matmul(p_d[:], s_ones_row_f[:],
                                              s_rowq[:, ts(tt, 512)], start=True,
                                              stop=True, tile_position=(0, 0)))
            for c in range(NCH):
                w0 = [(sT, tB), hA] if c == 0 else []
                vec(lambda e, c=c: e.tensor_mul(s_x32[:, c, :], s_x32[:, c, :],
                                                p_c[:]), waits=w0,
                    k=(1 if c == NCH - 1 else 0), nd=True)
            m1 = sV.n
            for c in range(NCH):
                vec(lambda e, c=c: e.tensor_add(s_x32[:, c, :], s_x32[:, c, :],
                                                p_d[:]),
                    waits=[(sV, m1)] if c == 0 else [],
                    k=(1 if c == NCH - 1 else 0), nd=True)
            m2 = sV.n
            for c in range(NCH):
                vec(lambda e, c=c, sl=sl: e.tensor_scalar(
                    s_xn[:, c, sl], s_x32[:, c, :],
                    s_lngb[:, 0, c:c + 1], s_lngb[:, 1, c:c + 1],
                    ALU.mult, ALU.add),
                    waits=[(sV, m2)] if c == 0 else [],
                    k=(1 if c == NCH - 1 else 0), nd=True)
        if taps:
            for tt in range(4):
                sl = slice(base + 512 * tt, base + 512 * (tt + 1))
                dmaH(tap_xn.rearrange("(c p) t -> p c t", p=128)[:, :, sl],
                     s_xn[:, :, sl], waits=[cur(sV)])
    flush()

    proj_mark = {"v": 0}
    for hp in range(8):
        ha, hb = 2 * hp, 2 * hp + 1
        colsl = ts(hp, 128)
        wload = [(sT, proj_mark["v"])] if proj_mark["v"] else [cur(sT)]
        gw = [dmaG(s_wq[:], Wq.rearrange("(c p) i -> p c i", p=128)[:, :, colsl],
                   waits=wload),
              dmaG(s_wk[:], Wk.rearrange("(c p) i -> p c i", p=128)[:, :, colsl]),
              dmaG(s_wv[:], Wv.rearrange("(c p) i -> p c i", p=128)[:, :, colsl])]
        q_ev = {}
        for tt in range(4):
            sl = ts(tt, 512)
            pq = p_a if tt % 2 == 0 else p_b
            for c in range(NCH):
                if c == 0:
                    w0 = ([(sS, q_ev[tt - 2])] if tt >= 2
                          else gw + [cur(sV), cur(sS)])
                else:
                    w0 = []
                km = [(sT, 1)] if c == NCH - 1 else []
                P_op("tensor", lambda e, c=c, sl=sl, pq=pq: e.matmul(
                    pq[:], s_wq[:, c, :], s_xn[:, c, sl], start=(c == 0),
                    stop=(c == NCH - 1), tile_position=(0, 0)), waits=w0, incs=km)
            tq = sT.n
            q_ev[tt] = sca(lambda e, sl=sl, pq=pq, hp=hp: e.activation(
                s_Q[:, sl], pq[:], AF.Copy, scale=s_qscale[:, hp:hp + 1]),
                waits=[(sT, tq)], nd=True)
        k_ev = {}
        v_ev = {}
        vt_cp = {}
        pcs = [p_c, p_qkA]
        pds = [p_d, p_qkB]

        def emit_vt(tt):
            for j in range(4):
                idx = 4 * tt + j
                pdx = pds[idx % 2]
                wt = [(sV, v_ev[tt])] if j == 0 else []
                if idx >= 2:
                    wt.append((sV, vt_cp[idx - 2]))
                tj = mm(lambda e, j=j, tt=tt, pdx=pdx: e.transpose(
                    pdx[:, 0:128], s_Vf[:, tt % 2, ts(j, 128)], s_identf[:]),
                    waits=wt)
                vt_cp[idx] = vec(lambda e, idx=idx, pdx=pdx:
                                 e.tensor_copy(s_Vt[:, idx, :],
                                               pdx[:, 0:128]),
                                 waits=[(sT, tj)], nd=True)

        for tt in range(8):
            sl = ts(tt, 512)
            pk_ = p_a if tt % 2 == 0 else p_b
            for c in range(NCH):
                if c == 0:
                    w0 = ([(sV, k_ev[tt - 2])] if tt >= 2
                          else [(sS, q_ev[tt + 2])])
                else:
                    w0 = []
                km = [(sT, 1)] if c == NCH - 1 else []
                P_op("tensor", lambda e, c=c, sl=sl, pk_=pk_: e.matmul(
                    pk_[:], s_wk[:, c, :], s_xn[:, c, sl], start=(c == 0),
                    stop=(c == NCH - 1), tile_position=(0, 0)), waits=w0, incs=km)
            tk = sT.n
            k_ev[tt] = vec(lambda e, sl=sl, pk_=pk_: e.tensor_copy(
                s_K[:, sl], pk_[:]), waits=[(sT, tk)], nd=True)
            pv_ = pcs[tt % 2]
            for c in range(NCH):
                if c == 0:
                    w0 = ([(sV, v_ev[tt - 2])] if tt >= 2 else [cur(sV)])
                else:
                    w0 = []
                km = [(sT, 1)] if c == NCH - 1 else []
                P_op("tensor", lambda e, c=c, sl=sl, pv_=pv_: e.matmul(
                    pv_[:, 0:512], s_wv[:, c, :], s_xn[:, c, sl], start=(c == 0),
                    stop=(c == NCH - 1), tile_position=(0, 0)), waits=w0, incs=km)
            tv = sT.n
            v_ev[tt] = vec(lambda e, tt=tt, pv_=pv_: e.tensor_copy(
                s_Vf[:, tt % 2, :], pv_[:, 0:512]), waits=[(sT, tv)], nd=True)
            if tt >= 1:
                emit_vt(tt - 1)
        emit_vt(7)

        proj_mark["v"] = sT.n
        norm_accum(lambda t2: s_Q[:, ts(t2, 512)], 4, 0, True)
        norm_accum(lambda t2: s_K[:, ts(t2, 512)], 8, 1, False)
        cb_v = bound_bias(s_cbias[:], s_sqslope[:, hp:hp + 1])
        zmm_prev = [0]

        for qt in range(4):
            qsl = ts(qt, 512)
            ga = dmaG(s_al_lhsT[:],
                      c_alibi_lhsT.rearrange("p (v k m) -> p v k m", v=4, k=32)
                      [:, qt, :, :], waits=[cur(sT)])
            exp_marks = {}
            zpv_marks = {}
            za_marks = {}

            def emit_zpv(kc):
                ring = kc % 2
                em = exp_marks[kc]
                zw = [(sS, em)]
                if kc == 0:
                    zw.append(cur(sV))
                mm0(lambda e, ring=ring, kc=kc: e.matmul(
                    p_a[0:64, :], s_Vt[:, kc, 0:64], s_eS[:, 0, ring, :],
                    start=(kc == 0), stop=(kc == 31), tile_position=(0, 0)),
                    waits=zw)
                P_op("tensor", lambda e, ring=ring, kc=kc: e.matmul(
                    p_b[64:128, :], s_Vt[:, kc, 64:128], s_eS[:, 1, ring, :],
                    start=(kc == 0), stop=(kc == 31), tile_position=(0, 64)),
                    incs=[(sT, 1)])
                zpv_marks[kc] = sT.n
                vw = [(sS, em)]
                if kc < 2:
                    vw.append((sT, zmm_prev[0]))
                sa = 2 * (kc % 2)
                sb = 2 * (kc % 2) + 1
                if kc < 2:
                    vec(lambda e, ring=ring, sa=sa: e.tensor_copy(
                        s_zacc[:, sa, :], s_eS[:, 0, ring, :]), waits=vw,
                        k=0, nd=True)
                    za_marks[kc] = vec(lambda e, ring=ring, sb=sb: e.tensor_copy(
                        s_zacc[:, sb, :], s_eS[:, 1, ring, :]), nd=True)
                else:
                    vw.append((sV, za_marks[kc - 2]))
                    vec(lambda e, ring=ring, sa=sa: e.tensor_add(
                        s_zacc[:, sa, :], s_zacc[:, sa, :], s_eS[:, 0, ring, :]),
                        waits=vw, k=0, nd=True)
                    za_marks[kc] = vec(lambda e, ring=ring, sb=sb: e.tensor_add(
                        s_zacc[:, sb, :], s_zacc[:, sb, :], s_eS[:, 1, ring, :]),
                        nd=True)

            for kc in range(32):
                ksl = ts(kc, 128)
                half = ts(kc % 2, 512)
                w0 = []
                if kc == 0:
                    w0 = [ga, (sV, cb_v), cur(sS), cur(sV), cur(sT)]
                elif kc >= 2:
                    w0 = [(sS, exp_marks[kc - 2])]
                mm0(lambda e, ksl=ksl, qsl=qsl, half=half: e.matmul(
                    p_qkA[:, half], s_K[0:64, ksl], s_Q[0:64, qsl],
                    start=True, stop=False, tile_position=(0, 0)), waits=w0)
                mm0(lambda e, ksl=ksl, qsl=qsl, half=half: e.matmul(
                    p_qkB[:, half], s_K[64:128, ksl], s_Q[64:128, qsl],
                    start=True, stop=False, tile_position=(64, 0)))
                straddle = (4 * qt <= kc <= 4 * qt + 3)
                mm0(lambda e, kc=kc, half=half, st=straddle: e.matmul(
                    p_qkA[:, half], s_al_lhsT[0:3, kc, :], s_al_rhs[0:3, :],
                    start=False, stop=not st, tile_position=(0, 0)))
                km = [] if straddle else [(sT, 1)]
                P_op("tensor", lambda e, kc=kc, half=half, st=straddle: e.matmul(
                    p_qkB[:, half], s_al_lhsT[32:35, kc, :], s_al_rhs[32:35, :],
                    start=False, stop=not st, tile_position=(32, 0)), incs=km)
                if straddle:
                    di = kc - 4 * qt
                    mm0(lambda e, di=di, half=half: e.matmul(
                        p_qkA[:, half], s_neg2id[:], s_R[:, di, :],
                        start=False, stop=True, tile_position=(0, 0)))
                    P_op("tensor", lambda e, di=di, half=half: e.matmul(
                        p_qkB[:, half], s_neg2id[:], s_R[:, di, :],
                        start=False, stop=True, tile_position=(0, 0)),
                        incs=[(sT, 1)])
                tqk = sT.n
                ring = kc % 2
                we = [(sT, tqk)]
                if kc >= 2:
                    we.append((sT, zpv_marks[kc - 2]))
                    we.append((sV, za_marks[kc - 2]))
                sca(lambda e, ring=ring, half=half, scA=float(SLOPES[ha]):
                    e.activation(
                    s_eS[:, 0, ring, :], p_qkA[:, half], AF.Exp,
                    bias=s_cbias[:], scale=scA), waits=we, nd=True)
                em = sca(lambda e, ring=ring, half=half, scB=float(SLOPES[hb]):
                    e.activation(
                    s_eS[:, 1, ring, :], p_qkB[:, half], AF.Exp,
                    bias=s_cbias[:], scale=scB), nd=True)
                exp_marks[kc] = em
                if kc >= 1:
                    emit_zpv(kc - 1)
            emit_zpv(31)
            mm(lambda e: e.matmul(p_c[0:1, :], s_ones_colf[:], s_zacc[:, 0, :],
                                  start=True, stop=False, tile_position=(0, 0)),
               waits=[(sV, za_marks[31])])
            mm(lambda e: e.matmul(p_c[0:1, :], s_ones_colf[:], s_zacc[:, 2, :],
                                  start=False, stop=True, tile_position=(0, 0)))
            mm(lambda e: e.matmul(p_d[32:33, :], s_ones_colf[:],
                                  s_zacc[:, 1, :], start=True, stop=False,
                                  tile_position=(0, 32)))
            mm(lambda e: e.matmul(p_d[32:33, :], s_ones_colf[:],
                                  s_zacc[:, 3, :], start=False, stop=True,
                                  tile_position=(0, 32)))
            zmm_prev[0] = sT.n
            tz = sT.n
            vec(lambda e, qt=qt: e.tensor_copy(
                s_zw[0:1, 512 * qt:512 * qt + 512], p_c[0:1, :]),
                waits=[(sT, tz)], nd=True)
            vec(lambda e, qt=qt: e.tensor_copy(
                s_zw[32:33, 512 * qt:512 * qt + 512], p_d[32:33, :]), nd=True)
            vec(lambda e, qsl=qsl, hp=hp: e.tensor_copy(s_U[0:64, hp, qsl],
                                                        p_a[0:64, :]), nd=True)
            vec(lambda e, qsl=qsl, hp=hp: e.tensor_copy(s_U[64:128, hp, qsl],
                                                        p_b[64:128, :]),
                nd=True)
        gza = dmaG(s_zpk[:, 0:16], s_zw[0:1, :], waits=[cur(sV)])
        gzb = dmaG(s_zpk[:, 16:32], s_zw[32:33, :])
        vz = vec(lambda e: e.reciprocal(s_zpk[:], s_zpk[:]), waits=[gza, gzb])
        gz2a = dmaG(s_zw[0:1, :], s_zpk[:, 0:16], waits=[(sV, vz)])
        gz2b = dmaG(s_zw[32:33, :], s_zpk[:, 16:32])
        for qt in range(4):
            qsl = ts(qt, 512)
            tba = mm(lambda e, qt=qt: e.matmul(
                p_c[0:64, :], s_ones_row_f[0:1, 0:64],
                s_zw[0:1, 512 * qt:512 * qt + 512], start=True, stop=True,
                tile_position=(0, 0)), waits=[gz2a, gz2b, cur(sV)])
            tbb = mm(lambda e, qt=qt: e.matmul(
                p_d[64:128, :], s_ones33[32:33, 0:64],
                s_zw[32:33, 512 * qt:512 * qt + 512], start=True, stop=True,
                tile_position=(32, 64)))
            vec(lambda e, hp=hp, qsl=qsl: e.tensor_mul(
                s_U[0:64, hp, qsl], s_U[0:64, hp, qsl], p_c[0:64, :]),
                waits=[(sT, tbb)] + ([cur(sV)] if qt == 0 else []), nd=True)
            vec(lambda e, hp=hp, qsl=qsl: e.tensor_mul(
                s_U[64:128, hp, qsl], s_U[64:128, hp, qsl], p_d[64:128, :]),
                nd=True)
    flush()

    for tt in range(4):
        sl = ts(tt, 512)
        hx = dmaH(s_x32[:], xfm3[:, :, sl], waits=[cur(sV), *HALL()])
        for d in range(NCH):
            pd_ = p_a if d % 2 == 0 else p_b
            gwo = dmaG(s_wo2[:, d % 2, :, :],
                       Wo.rearrange("(c p) d -> p c d", p=128)[:, :, ts(d, 128)],
                       waits=[cur(sT)])
            for ic in range(NCH):
                w0 = [gwo, cur(sV)] if ic == 0 else []
                km = [(sT, 1)] if ic == NCH - 1 else []
                P_op("tensor", lambda e, d=d, ic=ic, sl=sl, pd_=pd_: e.matmul(
                    pd_[:], s_wo2[:, d % 2, ic, :], s_U[:, ic, sl],
                    start=(ic == 0), stop=(ic == NCH - 1),
                    tile_position=(0, 0)), waits=w0, incs=km)
            tw = sT.n
            vec(lambda e, d=d, pd_=pd_: e.tensor_add(
                s_x32[:, d, :], s_x32[:, d, :], pd_[:]),
                waits=[(sT, tw), hx], nd=True)
        vr = sV.n
        dmaH(spill3[:, :, sl], s_x32[:], waits=[(sV, vr)])
        if taps:
            dmaH(tap_resid.rearrange("(c p) t -> p c t", p=128)[:, :, sl],
                 s_x32[:], waits=[(sV, vr)])
    flush()
    ab.close()
    barrier()

    if n_ffn >= 1:
        fd = ExitStack()
        FA = fd.enter_context
        s_xf = FA(nc.sbuf_tensor("s_xf", [128, NCH, 1 + T], F32))
        s_fqs = FA(nc.sbuf_tensor("s_fqs", [128, 2], F32))
        s_lgb2 = FA(nc.sbuf_tensor("s_lgb2", [128, 2, NCH], F32))
        s_b2f = FA(nc.sbuf_tensor("s_b2f", [128, NCH], F32))
        s_b1 = FA(nc.sbuf_tensor("s_b1", [128, 2, NAB], F32))
        s_fcb = FA(nc.sbuf_tensor("s_fcb", [128, 1], F32))

        def exch_col():
            g0 = dmaG(d_snd_col.ap(), s_xf[:, :, T:T + 1], waits=[cur(sV)], slow=True)
            P_op("gpsimd", lambda e: e.collective_compute(
                "AllGather", ALU.bypass,
                replica_groups=[[0, 1], [2, 3], [4, 5], [6, 7]],
                ins=[d_snd_col.ap().opt()], outs=[d_rcv_col.ap().opt()]),
                waits=[g0], incs=[(sCC, 1)])
            cc = sCC.n
            gr = dmaG(s_xcol[:], d_rcv_col.ap()[0:128, :], waits=[(sCC, cc)], slow=True)
            for c in range(NCH):
                vec(lambda e, c=c: e.tensor_mul(s_xf[:, c, 0:1], s_xcol[:, c, :],
                                                s_gate[:]),
                    waits=[gr] if c == 0 else [],
                    k=(1 if c == NCH - 1 else 0))
            return sV.n

        cs = ExitStack()
        s_xtc = cs.enter_context(nc.sbuf_tensor("s_xtc", [128, NCH, 512], BF16))
        for tt in range(4):
            sl = ts(tt, 512)
            gB = dmaG(s_xtc[:], spill3[:, :, sl],
                      waits=[cur(sV), cur(sT), *HALL()])
            stats_tile(lambda c: s_xtc[:, c, :],
                       s_rowm[:, ts(tt, 512)], s_rowq[:, ts(tt, 512)], [gB])
        g2 = ln_rows(16, double_ln=False)
        for tt in range(4):
            sl = ts(tt, 512)
            hA = dmaH(s_x32[:], spill3[:, :, sl], waits=[cur(sV), *HALL()])
            mm(lambda e, tt=tt: e.matmul(p_c[:], s_ones_row_f[:],
                                         s_rowm[:, ts(tt, 512)], start=True,
                                         stop=True, tile_position=(0, 0)),
               waits=list(g2) + [cur(sV)])
            tB = mm(lambda e, tt=tt: e.matmul(p_d[:], s_ones_row_f[:],
                                              s_rowq[:, ts(tt, 512)], start=True,
                                              stop=True, tile_position=(0, 0)))
            for c in range(NCH):
                w0 = [(sT, tB), hA] if c == 0 else []
                vec(lambda e, c=c: e.tensor_mul(s_x32[:, c, :], s_x32[:, c, :],
                                                p_c[:]), waits=w0,
                    k=(1 if c == NCH - 1 else 0), nd=True)
            m1 = sV.n
            for c in range(NCH):
                vec(lambda e, c=c, tt=tt: e.tensor_add(
                    s_xf[:, c, 1 + 512 * tt:1 + 512 * (tt + 1)],
                    s_x32[:, c, :], p_d[:]),
                    waits=[(sV, m1)] if c == 0 else [],
                    k=(1 if c == NCH - 1 else 0), nd=True)
        flush()
        cs.close()
        barrier()
        exch_col()
        flush()

        for l in range(n_ffn):
            Ls = ExitStack()
            s_fq = Ls.enter_context(nc.sbuf_tensor(f"s_fq{l}", [128, 4, T], BF16))
            s_fk = Ls.enter_context(
                nc.sbuf_tensor(f"s_fk{l}", [128, 4, WIN + T], BF16))
            s_fk2 = Ls.enter_context(
                nc.sbuf_tensor(f"s_fk2{l}", [128, 512], BF16))
            s_fv = Ls.enter_context(
                nc.sbuf_tensor(f"s_fv{l}", [128, 20, 512], BF16))

            vmarks = [dmaH(s_fqs[:], f_qsv[l], waits=[cur(sV)]),
                      dmaH(s_lgb2[:], f_lgb[l]),
                      dmaH(s_b2f[:], f_b2v[l]),
                      dmaH(s_b1[:], f_b1v[l])]
            vec(lambda e: e.nop(), waits=vmarks)
            sca(lambda e: e.nop(), waits=vmarks)

            sc1 = ExitStack()
            s_xt = sc1.enter_context(
                nc.sbuf_tensor(f"s_xt{l}", [128, NCH, 513], BF16))
            s_fwq = sc1.enter_context(
                nc.sbuf_tensor(f"s_fwq{l}", [128, NCH, 1024], BF16))

            gwq = dmaG(s_fwq[:], fw_qkv[l].rearrange(
                "(c p) i -> p c i", p=128)[:, :, 0:1024], waits=[cur(sT)])

            def xt_sl(c, t0, n):
                off = 0 if c >= 4 else 1
                return s_xt[:, c, off + t0: off + t0 + n]

            for tt in range(4):
                sl = ts(tt, 512)
                gxt = dmaG(s_xt[:], s_xf[:, :, 512 * tt:512 * tt + 513],
                           waits=[cur(sV), cur(sT)])
                for oc in range(8):
                    po = p_a if oc % 2 == 0 else p_b
                    for c in range(NCH):
                        w0 = [gwq, gxt, cur(sV)] if c == 0 else []
                        km = [(sT, 1)] if c == NCH - 1 else []
                        P_op("tensor", lambda e, oc=oc, c=c, po=po: e.matmul(
                            po[:], s_fwq[:, c, ts(oc, 128)], xt_sl(c, 0, 512),
                            start=(c == 0), stop=(c == NCH - 1),
                            tile_position=(0, 0)), waits=w0, incs=km)
                    tp = sT.n
                    if oc < 4:
                        vec(lambda e, oc=oc, sl=sl, po=po: e.tensor_copy(
                            s_fq[:, oc, sl], po[:]), waits=[(sT, tp)], nd=True)
                    else:
                        vec(lambda e, oc=oc, tt=tt, po=po: e.tensor_copy(
                            s_fk[:, oc - 4, WIN + 512 * tt:WIN + 512 * (tt + 1)],
                            po[:]), waits=[(sT, tp)], nd=True)
            gwv = dmaG(s_fwq[:, :, 0:512], fw_qkv[l].rearrange(
                "(c p) i -> p c i", p=128)[:, :, 1024:1536], waits=[cur(sT)])
            for tt in range(4):
                gxt = dmaG(s_xt[:], s_xf[:, :, 512 * tt:512 * tt + 513],
                           waits=[cur(sV), cur(sT)])
                for j in range(4):
                    tchunk = 4 * tt + j
                    for c in range(NCH):
                        w0 = [gwv, gxt, cur(sV)] if c == 0 else []
                        km = [(sT, 1)] if c == NCH - 1 else []
                        P_op("tensor", lambda e, c=c, j=j: e.matmul(
                            p_c[:], xt_sl(c, 128 * j, 128), s_fwq[:, c, 0:512],
                            start=(c == 0), stop=(c == NCH - 1),
                            tile_position=(0, 0)), waits=w0, incs=km)
                    tvv = sT.n
                    vec(lambda e, tchunk=tchunk: e.tensor_copy(
                        s_fv[:, 4 + tchunk, :], p_c[:]), waits=[(sT, tvv)],
                        nd=True)
            flush()
            sc1.close()
            barrier()

            sc2 = ExitStack()
            s_rot = {k: sc2.enter_context(
                nc.sbuf_tensor(f"s_rot_{k}{l}", [128, T], BF16))
                for k in ("qc", "qs", "kc", "ks")}
            s_rtmp = sc2.enter_context(nc.sbuf_tensor(f"s_rtmp{l}", [128, T], BF16))
            s_ttmp = sc2.enter_context(nc.sbuf_tensor(f"s_ttmp{l}", [128, T], BF16))
            s_rn = sc2.enter_context(nc.sbuf_tensor(f"s_rn{l}", [2, 2, 512], BF16))
            for k in ("qc", "qs", "kc", "ks"):
                dmaG(s_rot[k][:], c_rot[k])
            for pr in range(4):
                for tt in range(4):
                    sl = slice(512 * tt, 512 * tt + 512)
                    vsq = vec(lambda e, pr=pr, sl=sl: e.tensor_mul(
                        s_xsq[:, 0, :], s_fq[:, pr, sl], s_fq[:, pr, sl]))
                    tn = mm(lambda e: e.matmul(p_c[0:2, :], s_blk2[:],
                                               s_xsq[:, 0, :], start=True,
                                               stop=True, tile_position=(0, 0)),
                            waits=[(sV, vsq), cur(sS)])
                    vq = vec(lambda e: e.tensor_scalar_add(
                        p_c[0:2, :], p_c[0:2, :], 1e-24), waits=[(sT, tn)])
                    sln = sca(lambda e: e.activation(
                        p_c[0:2, :], p_c[0:2, :], AF.Ln),
                        waits=[(sV, vq)], nd=True)
                    sca(lambda e: e.activation(
                        s_rn[:, 0, :], p_c[0:2, :], AF.Exp, scale=-0.5),
                        waits=[(sS, sln)], nd=True)
                    vsq = vec(lambda e, pr=pr, sl=sl: e.tensor_mul(
                        s_xsq[:, 1, :], s_fk[:, pr, WIN + sl.start:WIN + sl.stop],
                        s_fk[:, pr, WIN + sl.start:WIN + sl.stop]))
                    tn = mm(lambda e: e.matmul(p_d[0:2, :], s_blk2[:],
                                               s_xsq[:, 1, :], start=True,
                                               stop=True, tile_position=(0, 0)),
                            waits=[(sV, vsq), cur(sS)])
                    vq = vec(lambda e: e.tensor_scalar_add(
                        p_d[0:2, :], p_d[0:2, :], 1e-24), waits=[(sT, tn)])
                    sln2 = sca(lambda e: e.activation(
                        p_d[0:2, :], p_d[0:2, :], AF.Ln),
                        waits=[(sV, vq)], nd=True)
                    smn = sca(lambda e: e.activation(
                        s_rn[:, 1, :], p_d[0:2, :], AF.Exp, scale=-0.5),
                        waits=[(sS, sln2)], nd=True)
                    tb = mm(lambda e: e.matmul(
                        p_c[:], s_ind2[:], s_rn[:, 0, :], start=True, stop=True,
                        tile_position=(0, 0)), waits=[(sS, smn), cur(sV)])
                    vec(lambda e, pr=pr, sl=sl: e.scalar_tensor_tensor(
                        s_fq[:, pr, sl], s_fq[:, pr, sl], s_fqs[:, 0:1], p_c[:],
                        ALU.mult, ALU.mult), waits=[(sT, tb)], nd=True)
                    tb2 = mm(lambda e: e.matmul(
                        p_d[:], s_ind2[:], s_rn[:, 1, :], start=True, stop=True,
                        tile_position=(0, 0)), waits=[cur(sV)])
                    vec(lambda e, pr=pr, sl=sl: e.scalar_tensor_tensor(
                        s_fk[:, pr, WIN + sl.start:WIN + sl.stop],
                        s_fk[:, pr, WIN + sl.start:WIN + sl.stop],
                        s_fqs[:, 1:2], p_d[:], ALU.mult, ALU.mult),
                        waits=[(sT, tb2)], nd=True)
                vnorm = sV.n
                swaps = ((0, 32), (32, 0), (64, 96), (96, 64))
                gq = []
                for dsts, srcs in swaps:
                    gq.append(dmaG(s_rtmp[dsts:dsts + 32, :],
                                   s_fq[srcs:srcs + 32, pr, :],
                                   waits=[(sV, vnorm)] if dsts == 0 else []))
                vec(lambda e, pr=pr: e.tensor_mul(s_ttmp[:], s_fq[:, pr, :],
                                                  s_rot["qc"][:]),
                    waits=gq, nd=True)
                m2_ = vec(lambda e: e.tensor_mul(s_rtmp[:], s_rtmp[:],
                                                 s_rot["qs"][:]), nd=True)
                vec(lambda e, pr=pr: e.tensor_add(s_fq[:, pr, :], s_ttmp[:],
                                                  s_rtmp[:]),
                    waits=[(sV, m2_)], nd=True)
                gk = []
                for dsts, srcs in swaps:
                    gk.append(dmaG(s_rtmp[dsts:dsts + 32, :],
                                   s_fk[srcs:srcs + 32, pr, WIN:WIN + T],
                                   waits=[(sV, sV.n)] if dsts == 0 else []))
                vec(lambda e, pr=pr: e.tensor_mul(
                    s_ttmp[:], s_fk[:, pr, WIN:WIN + T], s_rot["kc"][:]),
                    waits=gk, nd=True)
                m4_ = vec(lambda e: e.tensor_mul(s_rtmp[:], s_rtmp[:],
                                                 s_rot["ks"][:]), nd=True)
                vec(lambda e, pr=pr: e.tensor_add(s_fk[:, pr, WIN:WIN + T],
                                                  s_ttmp[:], s_rtmp[:]),
                    waits=[(sV, m4_)], nd=True)
            flush()
            sc2.close()
            barrier()

            vkv = sV.n
            snd_marks = []
            for p in range(4):
                snd_marks.append(dmaG(d_snd_kv.ap()[ts(p, 128), :],
                                      s_fk[:, p, WIN + T - 512:WIN + T],
                                      waits=[(sV, vkv)] if p == 0 else []))
            for p in range(4):
                snd_marks.append(dmaG(
                    d_snd_kv.ap()[512 + 128 * p:512 + 128 * (p + 1), :],
                    s_fv[:, 16 + p, :]))
            P_op("gpsimd", lambda e: e.collective_compute(
                "AllGather", ALU.bypass,
                replica_groups=[[0, 1], [2, 3], [4, 5], [6, 7]],
                ins=[d_snd_kv.ap().opt()], outs=[d_rcv_kv.ap().opt()]),
                waits=snd_marks, incs=[(sCC, 1)])
            ccm = sCC.n
            hal_marks = []
            for p in range(4):
                hal_marks.append(dmaG(s_fk[:, p, 0:WIN],
                                      d_rcv_kv.ap()[ts(p, 128), :],
                                      waits=[(sCC, ccm)] if p == 0 else []))
            for p in range(4):
                hal_marks.append(dmaG(
                    s_fv[:, p, :],
                    d_rcv_kv.ap()[512 + 128 * p:512 + 128 * (p + 1), :]))
            ghal = hal_marks
            flush()

            sc3 = ExitStack()
            s_feS = sc3.enter_context(
                nc.sbuf_tensor(f"s_feS{l}", [128, 4, 640], BF16))
            s_fU = sc3.enter_context(
                nc.sbuf_tensor(f"s_fU{l}", [128, 4, T], BF16))
            s_fwo = sc3.enter_context(
                nc.sbuf_tensor(f"s_fwo{l}", [128, 2, DM], BF16))
            s_zwf = sc3.enter_context(nc.sbuf_tensor(f"s_zwf{l}", [64, 2048], F32))
            s_zaccf = sc3.enter_context(
                nc.sbuf_tensor(f"s_zaccf{l}", [128, 4, 128], F32))
            s_zpk2 = sc3.enter_context(nc.sbuf_tensor(f"s_zpk2{l}", [128, 32], F32))
            vec(lambda e: e.nop(), waits=list(ghal))
            tq_ = mm(lambda e: e.transpose(p_c[0:2, 0:128], s_fqs[:], s_identf[:]),
                     waits=[cur(sS), cur(sV)])
            vec(lambda e: e.reduce_max(s_tiny[0:2, 0:1], p_c[0:2, 0:128],
                                       axis=AX.X, apply_absolute_value=True),
                waits=[(sT, tq_)])
            tp_ = mm(lambda e: e.matmul(p_c[0:1, 0:1], s_tiny[0:2, 0:1],
                                        s_tiny[0:2, 0:1], start=True, stop=True,
                                        tile_position=(0, 0)),
                     waits=[cur(sV)])
            vec(lambda e: e.tensor_scalar(s_trow[:, 1:2], p_c[0:1, 0:1], -4.0,
                                          -1.0, ALU.mult, ALU.add),
                waits=[(sT, tp_)])
            tb_ = mm(lambda e: e.matmul(p_c[:, 0:1], s_ones_row_f[:],
                                        s_trow[:, 1:2], start=True, stop=True,
                                        tile_position=(0, 0)), waits=[cur(sV)])
            fcb_v = vec(lambda e: e.tensor_copy(s_fcb[:], p_c[:, 0:1]),
                        waits=[(sT, tb_)])

            for pr in range(4):
                for w in range(4):
                    k2v = vec(lambda e, pr=pr, w=w: e.tensor_scalar_mul(
                        s_fk2[:], s_fk[:, pr, 512 * w:512 * w + 512],
                        s_base2[:]), waits=list(ghal) + [cur(sT)], nd=True)
                    exp_m = {}
                    zpv_m = {}
                    cmb_m = {}
                    zmmu = [0]

                    def emit_zpv_f(u, pr=pr, w=w):
                        em = exp_m[u]
                        slA = 2 * (u % 2)
                        slB = 2 * (u % 2) + 1
                        zam = {}
                        for kr in range(5):
                            kc = u + kr
                            esl = slice(128 * kr, 128 * kr + 128)
                            gstart = (u == 0 and kr == 0)
                            gstop = (u == 3 and kr == 4)
                            zw = [(sS, em)] if kr == 0 else []
                            if u == 0 and kr == 0:
                                zw.append(cur(sV))
                            vch = 4 * w + kc
                            mm0(lambda e, esl=esl, slA=slA, u=u, vch=vch, pr=pr,
                                gstart=gstart, gstop=gstop: e.matmul(
                                    p_a[0:64, 128 * u:128 * u + 128],
                                    s_fv[:, vch, 128 * pr:128 * pr + 64],
                                    s_feS[:, slA, esl],
                                    start=gstart, stop=gstop,
                                    tile_position=(0, 0)), waits=zw)
                            km = [(sT, 1)] if kr == 4 else []
                            P_op("tensor", lambda e, esl=esl, slB=slB, u=u,
                                 vch=vch, pr=pr, gstart=gstart, gstop=gstop:
                                 e.matmul(
                                     p_b[64:128, 128 * u:128 * u + 128],
                                     s_fv[:, vch, 128 * pr + 64:128 * pr + 128],
                                     s_feS[:, slB, esl],
                                     start=gstart, stop=gstop,
                                     tile_position=(0, 64)), incs=km)
                            sa = 2 * (kr % 2)
                            sb = sa + 1
                            vw = [(sS, em)] if kr == 0 else []
                            if kr < 2:
                                vw.append((sT, zmmu[0]))
                                vec(lambda e, esl=esl, slA=slA, sa=sa:
                                    e.tensor_copy(s_zaccf[:, sa, :],
                                                  s_feS[:, slA, esl]),
                                    waits=vw, k=0, nd=True)
                                zam[kr] = vec(lambda e, esl=esl, slB=slB, sb=sb:
                                              e.tensor_copy(s_zaccf[:, sb, :],
                                                            s_feS[:, slB, esl]),
                                              nd=True)
                            else:
                                vw.append((sV, zam[kr - 2]))
                                vec(lambda e, esl=esl, slA=slA, sa=sa:
                                    e.tensor_add(s_zaccf[:, sa, :],
                                                 s_zaccf[:, sa, :],
                                                 s_feS[:, slA, esl]),
                                    waits=vw, k=0, nd=True)
                                zam[kr] = vec(lambda e, esl=esl, slB=slB, sb=sb:
                                              e.tensor_add(s_zaccf[:, sb, :],
                                                           s_zaccf[:, sb, :],
                                                           s_feS[:, slB, esl]),
                                              nd=True)
                        vec(lambda e: e.tensor_add(s_zaccf[:, 0, :],
                                                   s_zaccf[:, 0, :],
                                                   s_zaccf[:, 2, :]),
                            waits=[(sV, zam[4])], k=0, nd=True)
                        cmb = vec(lambda e: e.tensor_add(s_zaccf[:, 1, :],
                                                         s_zaccf[:, 1, :],
                                                         s_zaccf[:, 3, :]),
                                  waits=[(sV, zam[4])], nd=True)
                        cmb_m[u] = cmb
                        mm0(lambda e, u=u: e.matmul(
                            p_c[0:1, 128 * u:128 * u + 128], s_ones_colf[:],
                            s_zaccf[:, 0, :], start=(u == 0), stop=(u == 3),
                            tile_position=(0, 0)), waits=[(sV, cmb)])
                        P_op("tensor", lambda e, u=u: e.matmul(
                            p_d[32:33, 128 * u:128 * u + 128], s_ones_colf[:],
                            s_zaccf[:, 1, :], start=(u == 0), stop=(u == 3),
                            tile_position=(0, 32)), incs=[(sT, 1)])
                        zmmu[0] = sT.n
                        zpv_m[u] = sT.n

                    for u in range(4):
                        for kr in range(5):
                            kc = u + kr
                            kb = 512 * w + 128 * kc
                            qb0 = 512 * w + 128 * u
                            first_mm = (u == 0 and kr == 0)
                            psl = slice(128 * kr, 128 * kr + 128)
                            gstart = (kr == 0) or (kr == 4)
                            mlist = []
                            if kr == 0:
                                mlist.append(1)
                            if kr == 4:
                                mlist.append(0)
                            if w == 0 and kc < 4:
                                mlist.append(2)
                            gstop = (kr == 3) or (kr == 4)
                            qk_stop = gstop and not mlist
                            w0 = []
                            if first_mm:
                                w0 = [(sV, fcb_v), cur(sS),
                                      cur(sV), cur(sT)] + list(ghal)
                            elif kr == 0 and u >= 1:
                                w0 = [(sS, exp_m[u - 1])]
                            if kc < 4:
                                kap_a = s_fk2[0:64, kb - 512 * w:kb - 512 * w + 128]
                                kap_b = s_fk2[64:128, kb - 512 * w:kb - 512 * w + 128]
                            else:
                                kap_a = s_fk[0:64, pr, kb:kb + 128]
                                kap_b = s_fk[64:128, pr, kb:kb + 128]
                            if first_mm or (kr == 0 and u == 0):
                                w0 = list(w0) + [(sV, k2v)]
                            mm0(lambda e, qb0=qb0, psl=psl, pr=pr,
                                gstart=gstart, qk_stop=qk_stop, kap_a=kap_a:
                                e.matmul(p_qkA[:, psl], kap_a,
                                         s_fq[0:64, pr, qb0:qb0 + 128],
                                         start=gstart, stop=qk_stop,
                                         tile_position=(0, 0)), waits=w0)
                            km = [(sT, 1)] if not mlist else []
                            P_op("tensor", lambda e, qb0=qb0, psl=psl, pr=pr,
                                 gstart=gstart, qk_stop=qk_stop, kap_b=kap_b:
                                 e.matmul(p_qkB[:, psl], kap_b,
                                          s_fq[64:128, pr, qb0:qb0 + 128],
                                          start=gstart, stop=qk_stop,
                                          tile_position=(64, 0)), incs=km)
                            for mi, m in enumerate(mlist):
                                is_last = (mi == len(mlist) - 1)
                                mstop = gstop and is_last
                                mm0(lambda e, m=m, psl=psl, mstop=mstop: e.matmul(
                                    p_qkA[:, psl], s_ident[:], s_masks[:, m, :],
                                    start=False, stop=mstop,
                                    tile_position=(0, 0)))
                                mk = [(sT, 1)] if is_last else []
                                P_op("tensor", lambda e, m=m, psl=psl,
                                     mstop=mstop: e.matmul(
                                         p_qkB[:, psl], s_ident[:],
                                         s_masks[:, m, :], start=False,
                                         stop=mstop, tile_position=(0, 0)),
                                     incs=mk)
                        slA = 2 * (u % 2)
                        slB = 2 * (u % 2) + 1
                        we = [(sT, sT.n)]
                        if u >= 2:
                            we.append((sT, zpv_m[u - 2]))
                            we.append((sV, cmb_m[u - 2]))
                        sca(lambda e, slA=slA: e.activation(
                            s_feS[:, slA, :], p_qkA[:, 0:640],
                            AF.Exp, bias=s_fcb[:], scale=1.0), waits=we,
                            nd=True)
                        em = sca(lambda e, slB=slB: e.activation(
                            s_feS[:, slB, :], p_qkB[:, 0:640],
                            AF.Exp, bias=s_fcb[:], scale=1.0), nd=True)
                        exp_m[u] = em
                        if u >= 1:
                            emit_zpv_f(u - 1)
                    emit_zpv_f(3)
                    tz = sT.n
                    wsl = ts(w, 512)
                    vec(lambda e, w=w: e.tensor_copy(
                        s_zwf[0:1, 512 * w:512 * w + 512], p_c[0:1, :]),
                        waits=[(sT, tz)], nd=True)
                    vec(lambda e, w=w: e.tensor_copy(
                        s_zwf[32:33, 512 * w:512 * w + 512], p_d[32:33, :]),
                        nd=True)
                    vec(lambda e, pr=pr, wsl=wsl: e.tensor_copy(
                        s_fU[0:64, pr, wsl], p_a[0:64, :]), nd=True)
                    vec(lambda e, pr=pr, wsl=wsl: e.tensor_copy(
                        s_fU[64:128, pr, wsl], p_b[64:128, :]), nd=True)
                gza = dmaG(s_zpk2[:, 0:16], s_zwf[0:1, :], waits=[cur(sV)])
                gzb = dmaG(s_zpk2[:, 16:32], s_zwf[32:33, :])
                vz = vec(lambda e: e.reciprocal(s_zpk2[:], s_zpk2[:]),
                         waits=[gza, gzb])
                gz2a = dmaG(s_zwf[0:1, :], s_zpk2[:, 0:16], waits=[(sV, vz)])
                gz2b = dmaG(s_zwf[32:33, :], s_zpk2[:, 16:32])
                for w in range(4):
                    wsl = ts(w, 512)
                    tba = mm(lambda e, w=w: e.matmul(
                        p_c[0:64, :], s_ones_row_f[0:1, 0:64],
                        s_zwf[0:1, 512 * w:512 * w + 512], start=True,
                        stop=True, tile_position=(0, 0)),
                        waits=[gz2a, gz2b, cur(sV)])
                    tbb = mm(lambda e, w=w: e.matmul(
                        p_d[64:128, :], s_ones33[32:33, 0:64],
                        s_zwf[32:33, 512 * w:512 * w + 512], start=True,
                        stop=True, tile_position=(32, 64)))
                    vec(lambda e, pr=pr, wsl=wsl: e.tensor_mul(
                        s_fU[0:64, pr, wsl], s_fU[0:64, pr, wsl], p_c[0:64, :]),
                        waits=[(sT, tbb)] + ([cur(sV)] if w == 0 else []),
                        nd=True)
                    vec(lambda e, pr=pr, wsl=wsl: e.tensor_mul(
                        s_fU[64:128, pr, wsl], s_fU[64:128, pr, wsl],
                        p_d[64:128, :]), nd=True)
            wbanks = [p_qkA[:, 0:512], p_qkA[:, 512:1024],
                      p_qkB[:, 0:512], p_qkB[:, 512:1024],
                      p_a[:], p_b[:], p_c[:], p_d[:]]
            wo_marks = {}
            for tt in range(4):
                sl = ts(tt, 512)
                for ic in range(4):
                    wslot = wo_marks.get((tt, ic - 2)) or wo_marks.get(
                        (tt - 1, ic + 2))
                    wload = [(sT, wslot)] if wslot else [cur(sT)]
                    gwo = dmaG(s_fwo[:, ic % 2, :],
                               fw_o[l].rearrange("(c p) d -> p c d",
                                                 p=128)[:, ic, :],
                               waits=wload)
                    for d in range(NCH):
                        w0 = [gwo, cur(sV)] if d == 0 else []
                        km = [(sT, 1)] if d == NCH - 1 else []
                        P_op("tensor", lambda e, d=d, ic=ic, sl=sl:
                             e.matmul(wbanks[d], s_fwo[:, ic % 2, ts(d, 128)],
                                      s_fU[:, ic, sl], start=(ic == 0),
                                      stop=(ic == 3), tile_position=(0, 0)),
                             waits=w0, incs=km)
                    wo_marks[(tt, ic)] = sT.n
                two = sT.n
                for d in range(NCH):
                    vec(lambda e, d=d, tt=tt: e.scalar_tensor_tensor(
                        s_xf[:, d, 1 + 512 * tt:1 + 512 * (tt + 1)],
                        wbanks[d], 0.0,
                        s_xf[:, d, 1 + 512 * tt:1 + 512 * (tt + 1)],
                        ALU.add, ALU.add),
                        waits=[(sT, two)] if d == 0 else [], nd=True)
            flush()
            sc3.close()
            barrier()
            Ls.close()
            barrier()
            exch_col()
            flush()

            sc4 = ExitStack()
            s_xt2 = sc4.enter_context(
                nc.sbuf_tensor(f"s_xt2{l}", [128, NCH, 513], BF16))
            s_h = sc4.enter_context(nc.sbuf_tensor(f"s_h{l}", [128, NAB, 512], BF16))
            s_w1 = sc4.enter_context(
                nc.sbuf_tensor(f"s_w1{l}", [128, 2, 2, NCH, 128], BF16))
            s_w2 = sc4.enter_context(
                nc.sbuf_tensor(f"s_w2{l}", [128, 2, NCH, 128], BF16))
            s_gg = sc4.enter_context(nc.sbuf_tensor(f"s_gg{l}", [128, 512], BF16))
            s_xnm = sc4.enter_context(
                nc.sbuf_tensor(f"s_xnm{l}", [128, NCH, T], BF16))

            def xt2_sl(c, t0, n):
                off = 0 if c >= 4 else 1
                return s_xt2[:, c, off + t0: off + t0 + n]

            for tt in range(4):
                gxt = dmaG(s_xt2[:], s_xf[:, :, 512 * tt:512 * tt + 513],
                           waits=[cur(sV), cur(sT)])
                stats_tile(lambda c: xt2_sl(c, 0, 512),
                           s_rowm[:, ts(tt, 512)], s_rowq[:, ts(tt, 512)],
                           [gxt])
            g2 = ln_rows(16, double_ln=False)
            for tt in range(4):
                sl = ts(tt, 512)
                mm(lambda e, tt=tt: e.matmul(p_c[:], s_ones_row_f[:],
                                             s_rowm[:, ts(tt, 512)], start=True,
                                             stop=True, tile_position=(0, 0)),
                   waits=list(g2) + [cur(sV)])
                tB = mm(lambda e, tt=tt: e.matmul(p_d[:], s_ones_row_f[:],
                                                  s_rowq[:, ts(tt, 512)],
                                                  start=True, stop=True,
                                                  tile_position=(0, 0)))
                for c in range(NCH):
                    off = 0 if c >= 4 else 1
                    w0 = [(sT, tB)] if c == 0 else []
                    vec(lambda e, c=c, tt=tt, off=off: e.tensor_mul(
                        s_x32[:, c, :],
                        s_xf[:, c, off + 512 * tt:off + 512 * tt + 512], p_c[:]),
                        waits=w0, k=(1 if c == NCH - 1 else 0), nd=True)
                m1 = sV.n
                for c in range(NCH):
                    vec(lambda e, c=c: e.tensor_add(s_x32[:, c, :],
                                                    s_x32[:, c, :], p_d[:]),
                        waits=[(sV, m1)] if c == 0 else [],
                        k=(1 if c == NCH - 1 else 0), nd=True)
                m2 = sV.n
                for c in range(NCH):
                    vec(lambda e, c=c, sl=sl: e.tensor_scalar(
                        s_xnm[:, c, sl], s_x32[:, c, :],
                        s_lgb2[:, 0, c:c + 1], s_lgb2[:, 1, c:c + 1],
                        ALU.mult, ALU.add),
                        waits=[(sV, m2)] if c == 0 else [],
                        k=(1 if c == NCH - 1 else 0), nd=True)
            w1_mm_marks = {}
            w2_mm_marks = {}
            for tt in range(4):
                sl = ts(tt, 512)
                vln = sV.n
                gel_marks = {}
                for k in range(NAB):
                    wdt = 128 if k < NAB - 1 else LAST
                    wslot = w1_mm_marks.get((tt, k - 2)) or w1_mm_marks.get(
                        (tt - 1, k + 20))
                    wload = [(sT, wslot)] if wslot else [cur(sT)]
                    ga_ = [dmaG(s_w1[:, k % 2, 0, :, 0:wdt],
                                fw_1[l].rearrange("(c p) i -> p c i", p=128)
                                [:, :, 128 * k:128 * k + wdt], waits=wload),
                           dmaG(s_w1[:, k % 2, 1, :, 0:wdt],
                                fw_1[l].rearrange("(c p) i -> p c i", p=128)
                                [:, :, HID + 128 * k:HID + 128 * k + wdt])]
                    hfA = ts(k % 2, 512)
                    for c in range(NCH):
                        w0 = ga_ + [(sV, vln)] if c == 0 else []
                        if k >= 2 and c == 0:
                            w0 = list(w0) + [(sS, gel_marks[k - 2][0]),
                                             (sV, gel_marks[k - 2][1])]
                        mm0(lambda e, k=k, c=c, wdt=wdt, hfA=hfA, sl=sl:
                            e.matmul(p_qkA[0:wdt, hfA], s_w1[:, k % 2, 0, c, 0:wdt],
                                     s_xnm[:, c, sl], start=(c == 0),
                                     stop=(c == NCH - 1), tile_position=(0, 0)),
                            waits=w0)
                        km = [(sT, 1)] if c == NCH - 1 else []
                        P_op("tensor", lambda e, k=k, c=c, wdt=wdt, hfA=hfA, sl=sl:
                             e.matmul(p_qkB[0:wdt, hfA], s_w1[:, k % 2, 1, c, 0:wdt],
                                      s_xnm[:, c, sl], start=(c == 0),
                                      stop=(c == NCH - 1), tile_position=(0, 0)),
                             incs=km)
                    tk_ = sT.n
                    w1_mm_marks[(tt, k)] = tk_
                    wg = [(sT, tk_)]
                    if k >= 1:
                        wg.append((sV, gel_marks[k - 1][1]))
                    sg = sca(lambda e, k=k, wdt=wdt, hfA=hfA: e.activation(
                        s_gg[0:wdt, :], p_qkB[0:wdt, hfA], AF.Gelu,
                        bias=s_b1[0:wdt, 1, k:k + 1], scale=1.0), waits=wg,
                        nd=True)
                    vg = vec(lambda e, k=k, wdt=wdt, hfA=hfA: e.scalar_tensor_tensor(
                        s_h[0:wdt, k, :], p_qkA[0:wdt, hfA],
                        s_b1[0:wdt, 0, k:k + 1],
                        s_gg[0:wdt, :], ALU.add, ALU.mult),
                        waits=[(sS, sg)], nd=True)
                    gel_marks[k] = (sg, vg)
                vh = sV.n
                banks = [p_qkA[:, 0:512], p_qkA[:, 512:1024],
                         p_qkB[:, 0:512], p_qkB[:, 512:1024],
                         p_a[:], p_b[:], p_c[:], p_d[:]]
                for k in range(NAB):
                    wdt = 128 if k < NAB - 1 else LAST
                    wslot = w2_mm_marks.get((tt, k - 2)) or w2_mm_marks.get(
                        (tt - 1, k + 20))
                    wload = [(sT, wslot)] if wslot else [cur(sT)]
                    gw2 = dmaG(s_w2[0:wdt, k % 2, :, :],
                               fw_2[l][128 * k:128 * k + wdt, :].rearrange(
                                   "p (c d) -> p c d", d=128), waits=wload)
                    for d in range(NCH):
                        w0 = [gw2, (sV, vh)] if d == 0 else []
                        km = [(sT, 1)] if (d == NCH - 1) else []
                        P_op("tensor", lambda e, k=k, d=d, wdt=wdt: e.matmul(
                            banks[d], s_w2[0:wdt, k % 2, d, :], s_h[0:wdt, k, :],
                            start=(k == 0), stop=(k == NAB - 1),
                            tile_position=(0, 0)), waits=w0, incs=km)
                tw2 = sT.n
                for d in range(NCH):
                    vec(lambda e, d=d, tt=tt: e.scalar_tensor_tensor(
                        s_xf[:, d, 1 + 512 * tt:1 + 512 * (tt + 1)],
                        banks[d], s_b2f[:, d:d + 1],
                        s_xf[:, d, 1 + 512 * tt:1 + 512 * (tt + 1)],
                        ALU.add, ALU.add), waits=[(sT, tw2)] if d == 0 else [],
                        k=(1 if d == NCH - 1 else 0), nd=True)
            flush()
            sc4.close()
            barrier()
            if l + 1 < n_ffn:
                exch_col()
                flush()

        for tt in range(4):
            sl = ts(tt, 512)
            hA = dmaH(s_x32[:], spill3[:, :, sl], waits=[cur(sV), *HALL()])
            for c in range(NCH):
                w0 = [hA] if c == 0 else []
                vec(lambda e, c=c, tt=tt: e.tensor_add(
                    s_x32[:, c, :], s_x32[:, c, :],
                    s_xf[:, c, 1 + 512 * tt:1 + 512 * (tt + 1)]),
                    waits=w0, k=(1 if c == NCH - 1 else 0), nd=True)
            vv = sV.n
            dmaH(out3[:, :, sl], s_x32[:], waits=[(sV, vv)])
        flush()
        fd.close()
    else:
        for tt in range(4):
            sl = ts(tt, 512)
            hA = dmaH(s_x32[:], spill3[:, :, sl], waits=[cur(sV), *HALL()])
            dmaH(out3[:, :, sl], s_x32[:], waits=[hA])
        flush()

    P_op("sync", lambda e: e.nop(), waits=HALL() + GALL())
    P_op("gpsimd", lambda e: e.nop(), waits=HALL() + GALL())
    flush()

    es.close()
    return nc



def kernel(**inputs):
    x = np.asarray(inputs["L0_attn_x"], np.float32)
    n_cores = 8
    nc = build()
    def pck(v):
        return np.ascontiguousarray(np.asarray(v, np.float32).reshape(NCH, 128).T)

    shared = {
        "ln_gb": np.ascontiguousarray(np.stack(
            [pck(inputs["L0_attn_ln_gamma"]), pck(inputs["L0_attn_ln_beta"])], axis=1)),
        "Wq": np.ascontiguousarray(np.asarray(inputs["L0_attn_Wq"], np.float32)),
        "Wk": np.ascontiguousarray(np.asarray(inputs["L0_attn_Wk"], np.float32)),
        "Wv": np.ascontiguousarray(np.asarray(inputs["L0_attn_Wv"], np.float32)),
        "Wo": np.ascontiguousarray(np.asarray(inputs["L0_attn_Wo"], np.float32)),
    }
    for l in range(2):
        shared[f"fw_qkv{l}"] = np.ascontiguousarray(np.asarray(inputs["L0_ffn_wqkv"], np.float32)[l])
        qsl_ = np.asarray(inputs["L0_ffn_q_scale"], np.float32)[l]
        ksl_ = np.asarray(inputs["L0_ffn_k_scale"], np.float32)[l]
        shared[f"f_qsv{l}"] = np.ascontiguousarray(np.stack(
            [np.concatenate([qsl_, qsl_]), np.concatenate([ksl_, ksl_])], axis=1))
        shared[f"fw_o{l}"] = np.ascontiguousarray(np.asarray(inputs["L0_ffn_wo"], np.float32)[l])
        shared[f"f_lgb{l}"] = np.ascontiguousarray(np.stack(
            [pck(np.asarray(inputs["L0_ffn_ln_g"], np.float32)[l]),
             pck(np.asarray(inputs["L0_ffn_ln_b"], np.float32)[l])], axis=1))
        shared[f"fw_1{l}"] = np.ascontiguousarray(np.asarray(inputs["L0_ffn_w1"], np.float32)[l])
        b1l = np.asarray(inputs["L0_ffn_b1"], np.float32)[l]
        b1arr = np.zeros((128, 2, NAB), np.float32)
        for half_ in range(2):
            v_ = b1l[half_ * HID:(half_ + 1) * HID]
            b1arr[:, half_, 0:21] = v_[0:2688].reshape(21, 128).T
            b1arr[0:LAST, half_, 21] = v_[2688:2730]
        shared[f"f_b1v{l}"] = np.ascontiguousarray(b1arr)
        shared[f"fw_2{l}"] = np.ascontiguousarray(np.asarray(inputs["L0_ffn_w2"], np.float32)[l])
        shared[f"f_b2v{l}"] = np.ascontiguousarray(pck(np.asarray(inputs["L0_ffn_b2"], np.float32)[l]))
    in_maps = []
    for c in range(n_cores):
        b, hh = c // 2, c % 2
        xb = x[b]
        own = xb[T * hh:T * hh + T].T
        oth = xb[T * (1 - hh):T * (1 - hh) + T].T
        d = dict(shared)
        d["xfm"] = np.ascontiguousarray(np.concatenate([own, oth], axis=1))
        d.update(host_constants(c))
        in_maps.append(d)
    res = run_bass_kernel_spmd(nc, in_maps, core_ids=list(range(n_cores)))
    out = np.empty((B, S, DM), np.float32)
    for c in range(n_cores):
        b, hh = c // 2, c % 2
        out[b, T * hh:T * hh + T] = res.results[c]["out_fm"].T
    return out
